# version 04 (final)
# baseline (speedup 1.0000x reference)
import numpy as np
import ml_dtypes
import concourse.bass as bass
import concourse.mybir as mybir
from concourse.bass_utils import run_bass_kernel_spmd
from contextlib import ExitStack

F32 = mybir.dt.float32
BF16 = mybir.dt.bfloat16
AF = mybir.ActivationFunctionType
ALU = mybir.AluOpType
AXX = mybir.AxisListType.X
BF = ml_dtypes.bfloat16

B, T, D = 2, 8192, 1024
NCORE = 8
UT = 1024
NU = 2
CT = NU * UT
NB = CT // 128
H, DK, DV = 16, 64, 64
EXP_OFF = 24.0


def units_for_core(i):
    b, c = i // 4, 2 * (i % 4)
    return [(b, c), (b, c + 1)]


def ts(i, n):
    return slice(i * n, (i + 1) * n)



ENGS = ("sync", "scalar", "vector", "tensor", "gpsimd")


class Prog:

    def __init__(self, nc, stack):
        self.nc = nc
        self.stack = stack
        self.q = {e: [] for e in ENGS}
        self.sem = {}
        self.cnt = {}
        for name in ("SC", "VE", "TP", "GP", "LD", "SO", "LDG", "CC"):
            self.sem[name] = stack.enter_context(nc.semaphore(name))
            self.cnt[name] = 0
        self.qsem = {"scalar": "SC", "vector": "VE", "tensor": "TP", "gpsimd": "GP"}
        self.writers = {}
        self.readers = {}
        self.hw = {e: {} for e in ENGS}
        self.bounds = {}

    DMA_SEMS = {"LD": "sync", "SO": "sync", "LDG": "gpsimd", "CC": "gpsimd"}

    def _boundary(self, semname, cnt):
        bl = self.bounds.setdefault(semname, [])
        for b in bl:
            if b >= cnt:
                return b
        qeng = self.DMA_SEMS[semname]
        T = self.cnt[semname]
        self.hw[qeng][semname] = max(self.hw[qeng].get(semname, 0), T)
        self.q[qeng].append(lambda e, s=self.sem[semname], v=T: e.wait_ge(s, v))
        bl.append(T)
        return T

    def _wait(self, eng, semname, cnt):
        if cnt <= 0:
            return
        if eng == "tensor" and semname == "TP":
            return
        if semname in self.DMA_SEMS:
            cnt = self._boundary(semname, cnt)
        if self.hw[eng].get(semname, 0) >= cnt:
            return
        self.hw[eng][semname] = cnt
        self.q[eng].append(lambda e, s=self.sem[semname], v=cnt: e.wait_ge(s, v))

    def step(self, eng, fn, reads=(), writes=(), sem=None, amt=1, extra_waits=()):
        need = {}
        for k in reads:
            for s, c in self.writers.get(k, {}).items():
                need[s] = max(need.get(s, 0), c)
        for k in writes:
            for s, c in self.writers.get(k, {}).items():
                need[s] = max(need.get(s, 0), c)
            for s, c in self.readers.get(k, {}).items():
                need[s] = max(need.get(s, 0), c)
        for s, c in extra_waits:
            need[s] = max(need.get(s, 0), c)
        semname = sem or self.qsem[eng]
        for s, c in sorted(need.items()):
            self._wait(eng, s, c)
        self.cnt[semname] += amt
        cnt = self.cnt[semname]
        def run(e, fn=fn, s=self.sem[semname], a=amt):
            fn(e).then_inc(s, a)
        self.q[eng].append(run)
        for k in writes:
            self.writers[k] = {semname: cnt}
            self.readers[k] = {}
        for k in reads:
            d = self.readers.setdefault(k, {})
            d[semname] = cnt
        return cnt

    def _queue_fence(self, eng):
        if eng == "sync":
            ew = [("LD", self.cnt["LD"]), ("SO", self.cnt["SO"])]
        else:
            ew = [("LDG", self.cnt["LDG"]), ("CC", self.cnt["CC"])]
        return [(s, c) for (s, c) in ew if c > 0]

    def dma(self, eng, out, in_, reads=(), writes=(), **kw):
        semname = {"sync": "LD", "gpsimd": "LDG"}[eng]
        return self.step(eng, lambda e: e.dma_start(out=out, in_=in_, **kw),
                         reads=reads, writes=writes, sem=semname, amt=16)

    def store(self, out, in_, reads=()):
        return self.step("sync", lambda e: e.dma_start(out=out, in_=in_),
                         reads=reads, writes=(), sem="SO", amt=16)

    def done(self):
        for e in ENGS:
            self._wait(e, "SO", self.cnt["SO"])

    def stage_barrier(self):
        for e in ENGS:
            for sname, c in self.cnt.items():
                self._wait(e, sname, c)

    def finalize(self):
        with self.nc.Block() as blk:
            for name in ENGS:
                deco = getattr(blk, name)
                def body(eng, name=name):
                    for fn in self.q[name]:
                        fn(eng)
                deco(body)


def mkbass():
    return bass.Bass("TRN2", target_bir_lowering=False)


class Fz:

    def __init__(self):
        self.nc = mkbass()
        self.gstack = ExitStack()
        self.p = None
        self.maps = {}
        self.stage_idx = 0

    def io(self, name, shape, dt, kind):
        ap = self.maps.get(name)
        if ap is None:
            raise KeyError(f"fused stage needs io mapping for {name}")
        return ap


def _fz_begin(fz):
    if fz is None:
        return mkbass(), ExitStack(), None
    fz.stage_idx += 1
    return fz.nc, ExitStack(), fz


def _fio(fz, nc, name, shape, dt, kind):
    if fz is None:
        return nc.dram_tensor(name, shape, dt, kind=kind).ap()
    return fz.io(name, shape, dt, kind)


def _nm(fz, nm):
    return nm if fz is None else f"g{fz.stage_idx}_{nm}"


def _fz_prog(fz, nc, stack):
    return Prog(nc, stack) if fz is None else fz.p


def _fz_end(fz, p):
    if fz is None:
        p.done()
        p.finalize()
    else:
        p.stage_barrier()


class ConstPool:

    def __init__(self, nc, stack, p, n=32, tag="cp"):
        self.tile = stack.enter_context(nc.sbuf_tensor(tag, [128, n], F32))
        self.p = p
        self.vals = {}
        self.tag = tag

    def get(self, val, rows=128):
        if val not in self.vals:
            j = len(self.vals)
            self.vals[val] = j
            self.p.step("vector", lambda e, j=j, val=val: e.memset(self.tile[:, j:j + 1], float(val)),
                        writes=[(self.tag, val)])
        j = self.vals[val]
        return self.tile[0:rows, j:j + 1], (self.tag, val)



def emit_ln(p, x_ap, xkey, out_bf, okey, scr, out2=None, o2key=None, eps=1e-5):
    t = scr["tag"]
    eps_ap, eps_key = scr["cpool"].get(eps, scr.get("rows", 128))
    xkey = xkey if isinstance(xkey, list) else [xkey]
    p.step("scalar", lambda e: e.activation(out=scr["sq"][:], in_=x_ap, func=AF.Square,
                                            accum_out=scr["s2"][:]),
           reads=xkey, writes=[t + "sq", t + "s2"] + ([scr["sqkey"]] if scr.get("sqkey") else []))
    p.step("vector", lambda e: e.reduce_sum(out=scr["s1"][:], in_=x_ap, axis=AXX),
           reads=xkey, writes=[t + "s1"])
    p.step("vector", lambda e: e.tensor_scalar(out=scr["m"][:], in0=scr["s1"][:],
                                               scalar1=1.0 / D, scalar2=None, op0=ALU.mult),
           reads=[t + "s1"], writes=[t + "m"])
    p.step("vector", lambda e: e.tensor_scalar(out=scr["var"][:], in0=scr["s2"][:],
                                               scalar1=1.0 / D, scalar2=None, op0=ALU.mult),
           reads=[t + "s2"], writes=[t + "var"])
    p.step("vector", lambda e: e.tensor_tensor(out=scr["m2"][:], in0=scr["m"][:],
                                               in1=scr["m"][:], op=ALU.mult),
           reads=[t + "m"], writes=[t + "m2"])
    p.step("vector", lambda e: e.tensor_tensor(out=scr["var"][:], in0=scr["var"][:],
                                               in1=scr["m2"][:], op=ALU.subtract),
           reads=[t + "var", t + "m2"], writes=[t + "var"])
    p.step("scalar", lambda e: e.activation(out=scr["lnv"][:], in_=scr["var"][:],
                                            func=AF.Ln, bias=eps_ap),
           reads=[t + "var", eps_key], writes=[t + "lnv"])
    p.step("scalar", lambda e: e.activation(out=scr["rstd"][:], in_=scr["lnv"][:],
                                            func=AF.Exp, scale=-0.5),
           reads=[t + "lnv"], writes=[t + "rstd"])
    p.step("vector", lambda e: e.scalar_tensor_tensor(out=scr["negmr"][:], in0=scr["m"][:],
                                                      scalar=-1.0, in1=scr["rstd"][:],
                                                      op0=ALU.mult, op1=ALU.mult),
           reads=[t + "m", t + "rstd"], writes=[t + "negmr"])
    p.step("scalar", lambda e: e.activation(out=out_bf, in_=x_ap, func=AF.Identity,
                                            bias=scr["negmr"][:, 0:1], scale=scr["rstd"][:, 0:1]),
           reads=xkey + [t + "negmr", t + "rstd"], writes=[okey])
    if out2 is not None:
        p.step("scalar", lambda e: e.activation(out=out2, in_=x_ap, func=AF.Identity,
                                                bias=scr["negmr"][:, 0:1], scale=scr["rstd"][:, 0:1]),
               reads=xkey + [t + "negmr", t + "rstd"], writes=[o2key])


def ln_scratch(nc, stack, tag="", cpool=None, rows=128, sq_tile=None, sq_key=None):
    sc = {"tag": tag, "cpool": cpool, "rows": rows}
    if sq_tile is None:
        sc["sq"] = stack.enter_context(nc.sbuf_tensor(f"{tag}lnsq", [rows, D], BF16))
        sc["sqkey"] = None
    else:
        sc["sq"] = sq_tile
        sc["sqkey"] = sq_key
    for nm in ("s1", "s2", "m", "m2", "var", "lnv", "rstd", "negmr"):
        sc[nm] = stack.enter_context(nc.sbuf_tensor(f"{tag}ln{nm}", [rows, 1], F32))
    return sc



def build_v1(fz=None):
    nc, stack, fz = _fz_begin(fz)
    xr = _fio(fz, nc, "xr", [CT, D], F32, "ExternalInput")
    wq = _fio(fz, nc, "wq", [H, DK, D], F32, "ExternalInput")
    wk = _fio(fz, nc, "wk", [H, DK, D], F32, "ExternalInput")
    wv = _fio(fz, nc, "wv", [H, DV, D], F32, "ExternalInput")
    ident_in = _fio(fz, nc, "ident", [128, 128], F32, "ExternalInput")
    qT_o = _fio(fz, nc, "qT", [H, DK, NU, UT], BF16, "ExternalOutput")
    kT_o = _fio(fz, nc, "kT", [H, DK, NU, UT], BF16, "ExternalOutput")
    va_o = _fio(fz, nc, "vaug", [CT, H * 65], BF16, "ExternalOutput")

    with stack:
        sb = lambda nm, shp, dt=BF16: stack.enter_context(nc.sbuf_tensor(_nm(fz, nm), shp, dt))
        psw = lambda nm, shp, dt=F32: stack.enter_context(nc.psum_tensor(_nm(fz, nm), shp, dt))
        ident = sb("identb", [128, 128])
        wtmp = sb("wtmp", [64, H * D])
        wT = {m: sb(f"w{m}T", [128, 8 * H * 64]) for m in "qkv"}
        xblk = [sb(f"xblk{j}", [128, D], F32) for j in range(2)]
        xnb = [sb(f"xnb{j}", [128, D]) for j in range(2)]
        xnT = sb("xnT", [128, 8 * CT])
        qTh = [sb(f"qTh{j}", [64, CT]) for j in range(2)]
        vb = [sb(f"vb{j}", [128, H * 65]) for j in range(2)]
        ptp = [psw(f"ptp{j}", [128, 1024], BF16) for j in range(2)]
        pmm = [psw(f"pmm{j}", [128, 512], F32) for j in range(2)]
        p = _fz_prog(fz, nc, stack)
        cpool = ConstPool(nc, stack, p, tag=_nm(fz, 'cp'))
        lnsc = ln_scratch(nc, stack, tag=_nm(fz, 'A'), cpool=cpool)

        p.dma("gpsimd", ident[:], ident_in[:], writes=["ident"])
        for m, w_in in zip("qkv", (wq, wk, wv)):
            for h in range(H):
                p.dma("gpsimd", wtmp[0:64, ts(h, D)], w_in[h],
                      writes=[("wtmp", h)])
            for h in range(H):
                for k in range(8):
                    pt = ptp[(h * 8 + k) % 2]
                    pk = ("ptp", (h * 8 + k) % 2)
                    p.step("tensor", lambda e, pt=pt, h=h, k=k: e.transpose(
                        out=pt[:, 0:64], in_=wtmp[0:64, h * D + 128 * k: h * D + 128 * (k + 1)],
                        identity=ident[0:64, 0:64]),
                        reads=[("wtmp", h), "ident"], writes=[pk])
                    p.step("vector", lambda e, pt=pt, m=m, h=h, k=k: e.tensor_copy(
                        out=wT[m][:, k * 1024 + h * 64: k * 1024 + (h + 1) * 64], in_=pt[:, 0:64]),
                        reads=[pk], writes=[("wT", m, h, k)])

        for i in range(NB):
            xb, xn = xblk[i % 2], xnb[i % 2]
            xbk, xnk = ("xblk", i % 2), ("xnb", i % 2)
            p.dma("sync", xb[:], xr[ts(i, 128), :], writes=[xbk])
            emit_ln(p, xb[:], xbk, xn[:], xnk, lnsc)
            for k in range(8):
                pt = ptp[k % 2]
                pk = ("ptp", k % 2)
                p.step("tensor", lambda e, pt=pt, xn=xn, k=k: e.transpose(
                    out=pt[:, 0:128], in_=xn[:, ts(k, 128)], identity=ident[:]),
                    reads=[xnk, "ident"], writes=[pk])
                p.step("vector", lambda e, pt=pt, i=i, k=k: e.tensor_copy(
                    out=xnT[:, k * CT + 128 * i: k * CT + 128 * (i + 1)], in_=pt[:, 0:128]),
                    reads=[pk], writes=[("xnT", k, i)])

        xnT_keys = [("xnT", k, i) for k in range(8) for i in range(NB)]
        for m, out_hbm, scale in (("q", qT_o, 0.125), ("k", kT_o, 1.0)):
            for h in range(H):
                qb = qTh[h % 2]
                qbk = ("qTh", h % 2)
                for tsl in range(4):
                    pm = pmm[tsl % 2]
                    pk = ("pmm", tsl % 2)
                    for k in range(8):
                        p.step("tensor", lambda e, pm=pm, m=m, h=h, k=k, tsl=tsl: e.matmul(
                            out=pm[0:64, :],
                            lhsT=wT[m][:, k * 1024 + h * 64: k * 1024 + (h + 1) * 64],
                            rhs=xnT[:, k * CT + 512 * tsl: k * CT + 512 * (tsl + 1)],
                            start=(k == 0), stop=(k == 7)),
                            reads=[("wT", m, h, k)] + [("xnT", k, i) for i in range(4 * tsl, 4 * tsl + 4)],
                            writes=[pk])
                    p.step("scalar", lambda e, pm=pm, qb=qb, tsl=tsl, scale=scale: e.activation(
                        out=qb[0:64, ts(tsl, 512)], in_=pm[0:64, :], func=AF.Copy, scale=scale),
                        reads=[pk], writes=[(qbk, tsl)])
                p.store(out_hbm[h], qb[0:64, :].rearrange("d (u t) -> d u t", u=NU),
                        reads=[(qbk, tsl) for tsl in range(4)])

        for j in range(2):
            p.step("vector", lambda e, j=j: e.memset(
                vb[j][:].rearrange("p (h e) -> p h e", e=65)[:, :, 64:65], 1.0),
                writes=[("vbones", j)])
        for i in range(NB):
            vbt = vb[i % 2]
            for j in range(2):
                pm = pmm[j % 2]
                pk = ("pmm", j % 2)
                for k in range(8):
                    p.step("tensor", lambda e, pm=pm, i=i, j=j, k=k: e.matmul(
                        out=pm[:],
                        lhsT=xnT[:, k * CT + 128 * i: k * CT + 128 * (i + 1)],
                        rhs=wT["v"][:, k * 1024 + 512 * j: k * 1024 + 512 * (j + 1)],
                        start=(k == 0), stop=(k == 7)),
                        reads=[("xnT", k, i)] + [("wT", "v", hh, k) for hh in range(8 * j, 8 * j + 8)],
                        writes=[pk])
                p.step("vector", lambda e, pm=pm, vbt=vbt, j=j: e.tensor_copy(
                    out=vbt[:].rearrange("p (h e) -> p h e", e=65)[:, 8 * j:8 * (j + 1), 0:64],
                    in_=pm[:].rearrange("p (h d) -> p h d", d=64)),
                    reads=[pk, ("vbones", i % 2)], writes=[("vb", i % 2, j)])
            p.store(va_o[ts(i, 128), :], vbt[:],
                    reads=[("vb", i % 2, 0), ("vb", i % 2, 1), ("vbones", i % 2)])
        _fz_end(fz, p)
    return nc



_IDENT = np.eye(128, dtype=np.float32)

_RUN = {"n": 0}


def _run(nc, in_maps):
    _RUN["n"] += 1
    return run_bass_kernel_spmd(nc, in_maps, core_ids=list(range(NCORE)))


def run_v1(X, inp):
    nc = build_v1()
    in_maps = []
    for i in range(NCORE):
        us = units_for_core(i)
        xr = np.concatenate([X[b, 1024 * c:1024 * (c + 1)] for (b, c) in us], axis=0)
        in_maps.append({
            "xr": np.ascontiguousarray(xr),
            "wq": np.asarray(inp["L0_attn_a_w_q"]), "wk": np.asarray(inp["L0_attn_a_w_k"]),
            "wv": np.asarray(inp["L0_attn_a_w_v"]), "ident": _IDENT,
        })
    res = _run(nc, in_maps)
    qT = np.zeros((B, H, DK, T), dtype=BF)
    kT = np.zeros((B, H, DK, T), dtype=BF)
    va = np.zeros((B, T, H * 65), dtype=BF)
    for i in range(NCORE):
        r = res.results[i]
        for ui, (b, c) in enumerate(units_for_core(i)):
            sl = slice(1024 * c, 1024 * (c + 1))
            qT[b, :, :, sl] = r["qT"][:, :, ui, :]
            kT[b, :, :, sl] = r["kT"][:, :, ui, :]
            va[b, sl] = r["vaug"][ts(ui, UT), :]
    return qT, kT, va





def build_v2(fz=None):
    nc, stack, fz = _fz_begin(fz)
    qT_in = _fio(fz, nc, "qTu", [NU, H, DK, UT], BF16, "ExternalInput")
    k_in = _fio(fz, nc, "kstr", [NU, H, DK, 8 * 1024], BF16, "ExternalInput")
    v_in = _fio(fz, nc, "vstr", [NU, 8 * 1024, H * 65], BF16, "ExternalInput")
    b_in = _fio(fz, nc, "bias", [1, NU * 8], F32, "ExternalInput")
    xr = _fio(fz, nc, "xr", [CT, D], F32, "ExternalInput")
    wo_in = _fio(fz, nc, "wo", [D, D], F32, "ExternalInput")
    ident_in = _fio(fz, nc, "ident", [128, 128], F32, "ExternalInput")
    x1_o = _fio(fz, nc, "x1r", [CT, D], F32, "ExternalOutput")

    with stack:
        sb = lambda nm, shp, dt=BF16: stack.enter_context(nc.sbuf_tensor(_nm(fz, nm), shp, dt))
        psw = lambda nm, shp, dt=F32: stack.enter_context(nc.psum_tensor(_nm(fz, nm), shp, dt))
        ident = sb("identb", [128, 128])
        qbuf = sb("qbuf", [64, NU * H * UT])
        kbuf = [sb(f"kbuf{j}", [64, 8 * 1024]) for j in range(2)]
        vbuf = [sb(f"vbuf{j}", [128, 64 * 65]) for j in range(2)]
        pbuf = [sb(f"pbuf{j}", [128, 1024]) for j in range(2)]
        osb = sb("osb", [128, 8 * 1024])
        oT = sb("oT", [128, 8 * 1024])
        wob = sb("wob", [128, 8 * 1024])
        xblk = [sb(f"xblk{j}", [128, D], F32) for j in range(2)]
        xout = [sb(f"xout{j}", [128, D], F32) for j in range(2)]
        ones_row = sb("ones_row", [1, 128], F32)
        braw = sb("braw", [1, NU * 8], F32)
        ebias = sb("ebias", [128, NU * 8], F32)
        rl = sb("rl", [128, 1], F32)
        sim_ps = [psw(f"simps{j}", [128, 1024]) for j in range(2)]
        pa = [psw(f"pa{j}", [128, 260]) for j in range(2)]
        ptp = psw("ptp", [128, 1024], BF16)
        pmm = psw("pmm", [128, 512])
        p = _fz_prog(fz, nc, stack)

        p.dma("gpsimd", ident[:], ident_in[:], writes=["ident"])
        p.dma("sync", braw[:], b_in[:], writes=["braw"])
        p.step("vector", lambda e: e.memset(ones_row[:], 1.0), writes=["ones"])
        p.step("tensor", lambda e: e.matmul(out=pmm[:, 0:NU * 8], lhsT=ones_row[:], rhs=braw[:],
                                            start=True, stop=True),
               reads=["ones", "braw"], writes=["pmm"])
        p.step("vector", lambda e: e.tensor_scalar(out=ebias[:], in0=pmm[:, 0:NU * 8], scalar1=-EXP_OFF,
                                                   scalar2=None, op0=ALU.add),
               reads=["pmm"], writes=["ebias"])
        for k in range(8):
            p.dma("gpsimd", wob[:, ts(k, 1024)], wo_in[ts(k, 128), :], writes=[("wob", k)])
        p.dma("sync", qbuf[0:64, :].rearrange("p (u h t) -> p u h t", u=NU, h=H),
              qT_in.rearrange("u h d t -> d u h t"), writes=["qbuf"])

        for u in range(NU):
            for h in range(H):
                i_uh = u * H + h
                kb_t, vb_t = kbuf[i_uh % 2], vbuf[i_uh % 2]
                kkey, vkey = ("kbuf", i_uh % 2), ("vbuf", i_uh % 2)
                p.dma("sync", kb_t[:], k_in[u, h], writes=[kkey])
                p.dma("sync", vb_t[:].rearrange("p (n c) -> p n c", c=65),
                      v_in[u, :, h * 65:(h + 1) * 65].rearrange("(n p) c -> p n c", p=128),
                      writes=[vkey])
                for j in range(8):
                    for kb in range(8):
                        sp = sim_ps[kb % 2]
                        spk = ("simps", kb % 2)
                        for qs in range(2):
                            p.step("tensor", lambda e, sp=sp, u=u, h=h, j=j, kb=kb, qs=qs, kb_t=kb_t: e.matmul(
                                out=sp[:, ts(qs, 512)],
                                lhsT=kb_t[0:64, j * 1024 + kb * 128: j * 1024 + (kb + 1) * 128],
                                rhs=qbuf[0:64, (u * H + h) * UT + qs * 512: (u * H + h) * UT + (qs + 1) * 512],
                                start=True, stop=True),
                                reads=[kkey, "qbuf"], writes=[(spk, qs)])
                        pb = pbuf[kb % 2]
                        pbk = ("pbuf", kb % 2)
                        p.step("scalar", lambda e, sp=sp, pb=pb, u=u, j=j: e.activation(
                            out=pb[:], in_=sp[:], func=AF.Exp, bias=ebias[:, u * 8 + j: u * 8 + j + 1]),
                            reads=[(spk, 0), (spk, 1), "ebias"], writes=[pbk])
                        if j == 0:
                            p.step("gpsimd", lambda e, pb=pb, kb=kb: e.affine_select(
                                out=pb[:], in_=pb[:], pattern=[[1, 1024]], base=-128 * kb,
                                channel_multiplier=-1, compare_op=ALU.is_ge, fill=0.0),
                                reads=[pbk], writes=[pbk])
                        for q8 in range(8):
                            pat = pa[q8 // 4]
                            p.step("tensor", lambda e, pat=pat, pb=pb, u=u, j=j, kb=kb, q8=q8, vb_t=vb_t: e.matmul(
                                out=pat[:, (q8 % 4) * 65:(q8 % 4) * 65 + 65],
                                lhsT=pb[:, ts(q8, 128)],
                                rhs=vb_t[:, (j * 8 + kb) * 65:(j * 8 + kb) * 65 + 65],
                                start=(j == 0 and kb == 0 and q8 % 4 == 0),
                                stop=(j == 7 and kb == 7 and q8 % 4 == 3)),
                                reads=[pbk, vkey], writes=[("pa", q8 // 4)])
                for q8 in range(8):
                    pat = pa[q8 // 4]
                    pak = ("pa", q8 // 4)
                    p.step("vector", lambda e, pat=pat, q8=q8: e.reciprocal(
                        out=rl[:], in_=pat[:, (q8 % 4) * 65 + 64:(q8 % 4) * 65 + 65]),
                        reads=[pak], writes=["rl"])
                    p.step("scalar", lambda e, pat=pat, q8=q8, h=h: e.activation(
                        out=osb[:, q8 * 1024 + h * 64: q8 * 1024 + (h + 1) * 64],
                        in_=pat[:, (q8 % 4) * 65:(q8 % 4) * 65 + 64],
                        func=AF.Identity, scale=rl[:, 0:1]),
                        reads=[pak, "rl"], writes=[("osb", q8, h)])
            for tb in range(8):
                for k in range(8):
                    p.step("tensor", lambda e, tb=tb, k=k: e.transpose(
                        out=ptp[:, 0:128], in_=osb[:, tb * 1024 + k * 128: tb * 1024 + (k + 1) * 128],
                        identity=ident[:]),
                        reads=[("osb", tb, 2 * k), ("osb", tb, 2 * k + 1), "ident"], writes=["ptp"])
                    p.step("vector", lambda e, tb=tb, k=k: e.tensor_copy(
                        out=oT[:, k * 1024 + tb * 128: k * 1024 + (tb + 1) * 128], in_=ptp[:, 0:128]),
                        reads=["ptp"], writes=[("oT", k, tb)])
            for tb in range(8):
                gi = u * 8 + tb
                xb, xo = xblk[tb % 2], xout[tb % 2]
                xbk, xok = ("xblk", tb % 2), ("xout", tb % 2)
                p.dma("sync", xb[:], xr[ts(gi, 128), :], writes=[xbk])
                for half in range(2):
                    for k in range(8):
                        p.step("tensor", lambda e, k=k, tb=tb, half=half: e.matmul(
                            out=pmm[:], lhsT=oT[:, k * 1024 + tb * 128: k * 1024 + (tb + 1) * 128],
                            rhs=wob[:, k * 1024 + half * 512: k * 1024 + (half + 1) * 512],
                            start=(k == 0), stop=(k == 7)),
                            reads=[("oT", k, tb), ("wob", k)], writes=["pmm"])
                    p.step("vector", lambda e, xb=xb, xo=xo, half=half: e.tensor_tensor(
                        out=xo[:, ts(half, 512)], in0=pmm[:], in1=xb[:, ts(half, 512)], op=ALU.add),
                        reads=["pmm", xbk], writes=[(xok, half)])
                p.store(x1_o[ts(gi, 128), :], xo[:], reads=[(xok, 0), (xok, 1)])
        _fz_end(fz, p)
    return nc


def host_attn_streams(qT, kT, va):
    maps = []
    for i in range(NCORE):
        us = units_for_core(i)
        qTu = np.stack([qT[b][:, :, 1024 * c:1024 * (c + 1)] for (b, c) in us])
        kstr = np.zeros((NU, H, DK, 8 * 1024), dtype=BF)
        vstr = np.zeros((NU, 8 * 1024, H * 65), dtype=BF)
        bias = np.zeros((1, NU * 8), dtype=np.float32)
        for ui, (b, c) in enumerate(us):
            order = list(range(c, -1, -1)) + [0] * (7 - c)
            kc = kT[b].reshape(H, DK, 8, 1024)
            kstr[ui] = kc[:, :, order].reshape(H, DK, 8 * 1024)
            vc = va[b].reshape(8, 1024, H * 65)
            vstr[ui] = vc[order].reshape(8 * 1024, H * 65)
            bias[0, ui * 8: ui * 8 + 8] = [0.0 if j <= c else -30000.0 for j in range(8)]
        maps.append({"qTu": qTu, "kstr": kstr, "vstr": vstr, "bias": bias})
    return maps



def build_v3(fz=None):
    nc, stack, fz = _fz_begin(fz)
    x1 = _fio(fz, nc, "x1r", [CT, D], F32, "ExternalInput")
    fw_in = _fio(fz, nc, "fw", [D, D], F32, "ExternalInput")
    fa_in = _fio(fz, nc, "falpha", [1, 1], F32, "ExternalInput")
    fb_in = _fio(fz, nc, "fbias", [1, D], F32, "ExternalInput")
    w1_in = _fio(fz, nc, "W1", [D, 6144], F32, "ExternalInput")
    b1_in = _fio(fz, nc, "b1", [6144, 1], F32, "ExternalInput")
    identf_in = _fio(fz, nc, "ident", [128, 128], F32, "ExternalInput")
    x2_o = _fio(fz, nc, "x2r", [CT, D], F32, "ExternalOutput")
    sh_o = _fio(fz, nc, "scanH", [NU, 2048, UT], F32, "ExternalOutput")
    sp_o = _fio(fz, nc, "scanP", [NU, 2048, UT], F32, "ExternalOutput")

    with stack:
        sb = lambda nm, shp, dt=BF16: stack.enter_context(nc.sbuf_tensor(_nm(fz, nm), shp, dt))
        psw = lambda nm, shp, dt=F32: stack.enter_context(nc.psum_tensor(_nm(fz, nm), shp, dt))
        identb = sb("identb", [128, 128])
        identf = sb("identf", [128, 128], F32)
        fwc = [sb(f"fwc{j}", [128, D], F32) for j in range(2)]
        awc = sb("awc", [128, D], F32)
        sgc = sb("sgc", [128, D], F32)
        cmpc = sb("cmpc", [128, D], F32)
        wtb = sb("wtb", [128, D])
        wtT = sb("wtT", [128, 8 * D])
        asums = sb("asums", [128, 9], F32)
        arow = sb("arow", [1, 128], F32)
        asc = sb("asc", [1, 2], F32)
        delta128 = sb("delta128", [128, 1], F32)
        alpha128 = sb("alpha128", [128, 1], F32)
        al = sb("al", [1, 2], F32)
        fbrow = sb("fbrow", [1, D], F32)
        fbsc = sb("fbsc", [1, D], F32)
        ones_row = sb("ones_row", [1, 128], F32)
        xblk = [sb(f"xblk{j}", [128, D], F32) for j in range(2)]
        x2blk = [sb(f"x2blk{j}", [128, D], F32) for j in range(2)]
        xnb = [sb(f"xnb{j}", [128, D]) for j in range(2)]
        xnTb = [sb(f"xnTb{j}", [128, D]) for j in range(2)]
        xnT2 = sb("xnT2", [128, 8 * CT])
        w1c = [sb(f"w1c{j}", [128, 8 * 128]) for j in range(2)]
        b1c = sb("b1c", [128, 48], F32)
        p3 = [sb(f"p3_{j}", [128, UT], F32) for j in range(3)]
        ew = {nm: stack.enter_context(nc.sbuf_tensor(_nm(fz, f"ew{nm}"), [128, UT], F32))
              for nm in ("ef", "ei", "den", "c", "vc", "s1", "r1", "g", "v")}
        ew["mask"] = stack.enter_context(nc.sbuf_tensor(_nm(fz, "ewmask"), [128, UT], mybir.dt.uint8))
        zeros = sb("zeros", [128, UT], F32)
        sch = sb("sch", [128, UT], F32)
        scp = sb("scp", [128, UT], F32)
        ptp = [psw(f"ptp{j}", [128, 1024], BF16) for j in range(2)]
        pmm = [psw(f"pmm{j}", [128, 512]) for j in range(2)]
        pp = [psw(f"pp{j}", [128, 512]) for j in range(2)]
        ptf = psw("ptf", [128, 128])
        p = _fz_prog(fz, nc, stack)
        cpool = ConstPool(nc, stack, p, tag=_nm(fz, 'cp'))
        lnsc = ln_scratch(nc, stack, tag=_nm(fz, 'A'), cpool=cpool)
        lnsc2 = ln_scratch(nc, stack, tag=_nm(fz, 'B'), cpool=cpool)

        p.dma("gpsimd", identb[:], identf_in[:], writes=["identb"])
        p.dma("sync", identf[:], identf_in[:], writes=["identf"])
        p.step("vector", lambda e: e.memset(ones_row[:], 1.0), writes=["ones"])
        p.step("vector", lambda e: e.memset(zeros[:], 0.0), writes=["zeros"])
        p.dma("sync", al[0:1, 0:1], fa_in[:], writes=["al0"])
        p.dma("sync", fbrow[:], fb_in[:], writes=["fbrow"])
        for j in range(48):
            p.dma("sync", b1c[:, j:j + 1], b1_in[ts(j, 128), :], writes=[("b1c", j)])

        for k in range(8):
            fw = fwc[k % 2]
            fk = ("fwc", k % 2)
            p.dma("sync", fw[:], fw_in[ts(k, 128), :], writes=[fk])
            p.step("vector", lambda e, fw=fw, k=k: e.reduce_sum(
                out=asums[:, k:k + 1], in_=fw[:], axis=AXX, apply_absolute_value=True),
                reads=[fk], writes=[("as", k)])
        p.step("vector", lambda e: e.reduce_sum(out=asums[:, 8:9], in_=asums[:, 0:8], axis=AXX),
               reads=[("as", k) for k in range(8)], writes=["as8"])
        p.step("tensor", lambda e: e.transpose(out=ptf[0:1, 0:128], in_=asums[:, 8:9],
                                               identity=identf[:]),
               reads=["as8", "identf"], writes=["ptf"])
        p.step("vector", lambda e: e.tensor_copy(out=arow[:], in_=ptf[0:1, 0:128]),
               reads=["ptf"], writes=["arow"])
        p.step("vector", lambda e: e.reduce_sum(out=asc[0:1, 0:1], in_=arow[:], axis=AXX),
               reads=["arow"], writes=["asc0"])
        p.step("vector", lambda e: e.tensor_scalar(out=asc[0:1, 1:2], in0=asc[0:1, 0:1],
                                                   scalar1=0.7 / (D * D), scalar2=None, op0=ALU.mult),
               reads=["asc0"], writes=["asc1"])
        p.step("tensor", lambda e: e.matmul(out=ptf[:, 0:1], lhsT=ones_row[:], rhs=asc[0:1, 1:2],
                                            start=True, stop=True),
               reads=["ones", "asc1"], writes=["ptf"])
        p.step("vector", lambda e: e.tensor_copy(out=delta128[:], in_=ptf[:, 0:1]),
               reads=["ptf"], writes=["delta128"])
        p.step("vector", lambda e: e.reciprocal(out=al[0:1, 1:2], in_=al[0:1, 0:1]),
               reads=["al0"], writes=["al1"])
        p.step("vector", lambda e: e.tensor_scalar(out=fbsc[:], in0=fbrow[:], scalar1=al[0:1, 1:2],
                                                   scalar2=None, op0=ALU.mult),
               reads=["fbrow", "al1"], writes=["fbsc"])
        p.step("tensor", lambda e: e.matmul(out=ptf[:, 1:2], lhsT=ones_row[:], rhs=al[0:1, 0:1],
                                            start=True, stop=True),
               reads=["ones", "al0", "delta128"], writes=["ptf"])
        p.step("vector", lambda e: e.tensor_copy(out=alpha128[:], in_=ptf[:, 1:2]),
               reads=["ptf"], writes=["alpha128"])
        for k in range(8):
            fw = fwc[k % 2]
            fk = ("fwc", k % 2)
            p.dma("sync", fw[:], fw_in[ts(k, 128), :], writes=[fk])
            p.step("scalar", lambda e, fw=fw: e.activation(out=awc[:], in_=fw[:], func=AF.Abs),
                   reads=[fk], writes=["awc"])
            p.step("scalar", lambda e, fw=fw: e.activation(out=sgc[:], in_=fw[:], func=AF.Sign),
                   reads=[fk], writes=["sgc"])
            p.step("vector", lambda e: e.tensor_single_scalar(out=cmpc[:], in_=awc[:],
                                                              scalar=delta128[:, 0:1], op=ALU.is_gt),
                   reads=["awc", "delta128"], writes=["cmpc"])
            p.step("vector", lambda e: e.tensor_tensor(out=wtb[:], in0=cmpc[:], in1=sgc[:], op=ALU.mult),
                   reads=["cmpc", "sgc"], writes=["wtb"])
            for kk in range(8):
                pt = ptp[kk % 2]
                ptk = ("ptp", kk % 2)
                p.step("tensor", lambda e, pt=pt, kk=kk: e.transpose(
                    out=pt[:, 0:128], in_=wtb[:, ts(kk, 128)], identity=identb[:]),
                    reads=["wtb", "identb"], writes=[ptk])
                p.step("vector", lambda e, pt=pt, k=k, kk=kk: e.tensor_copy(
                    out=wtT[:, kk * 1024 + k * 128: kk * 1024 + (k + 1) * 128], in_=pt[:, 0:128]),
                    reads=[ptk], writes=[("wtT", kk, k)])

        for i in range(NB):
            xb, x2b, xn, xnT_b = xblk[i % 2], x2blk[i % 2], xnb[i % 2], xnTb[i % 2]
            xbk, x2k, xnk, xnTk = ("xblk", i % 2), ("x2blk", i % 2), ("xnb", i % 2), ("xnTb", i % 2)
            p.dma("sync", xb[:], x1[ts(i, 128), :], writes=[xbk])
            emit_ln(p, xb[:], xbk, xn[:], xnk, lnsc)
            for k in range(8):
                pt = ptp[k % 2]
                ptk = ("ptp", k % 2)
                p.step("tensor", lambda e, pt=pt, xn=xn, k=k: e.transpose(
                    out=pt[:, 0:128], in_=xn[:, ts(k, 128)], identity=identb[:]),
                    reads=[xnk, "identb"], writes=[ptk])
                p.step("vector", lambda e, pt=pt, xnT_b=xnT_b, k=k: e.tensor_copy(
                    out=xnT_b[:, ts(k, 128)], in_=pt[:, 0:128]),
                    reads=[ptk], writes=[(xnTk, k)])
            for half in range(2):
                pm = pmm[half % 2]
                pmk = ("pmm", half % 2)
                p.step("tensor", lambda e, pm=pm, half=half: e.matmul(
                    out=pm[:], lhsT=ones_row[:], rhs=fbsc[0:1, ts(half, 512)],
                    start=True, stop=False),
                    reads=["ones", "fbsc"], writes=[pmk])
                for k in range(8):
                    p.step("tensor", lambda e, pm=pm, xnT_b=xnT_b, k=k, half=half: e.matmul(
                        out=pm[:], lhsT=xnT_b[:, ts(k, 128)],
                        rhs=wtT[:, k * 1024 + half * 512: k * 1024 + (half + 1) * 512],
                        start=False, stop=(k == 7)),
                        reads=[(xnTk, k)] + [("wtT", k, kk) for kk in range(8)], writes=[pmk])
                p.step("vector", lambda e, pm=pm, x2b=x2b, xb=xb, half=half: e.scalar_tensor_tensor(
                    out=x2b[:, ts(half, 512)], in0=pm[:], scalar=alpha128[:, 0:1],
                    in1=xb[:, ts(half, 512)], op0=ALU.mult, op1=ALU.add),
                    reads=[pmk, xbk, "alpha128"], writes=[(x2k, half)])
            p.store(x2_o[ts(i, 128), :], x2b[:], reads=[(x2k, 0), (x2k, 1)])
            emit_ln(p, x2b[:], [(x2k, 0), (x2k, 1)], xn[:], xnk, lnsc2)
            for k in range(8):
                pt = ptp[k % 2]
                ptk = ("ptp", k % 2)
                p.step("tensor", lambda e, pt=pt, xn=xn, k=k: e.transpose(
                    out=pt[:, 0:128], in_=xn[:, ts(k, 128)], identity=identb[:]),
                    reads=[xnk, "identb"], writes=[ptk])
                p.step("vector", lambda e, pt=pt, i=i, k=k: e.tensor_copy(
                    out=xnT2[:, k * CT + 128 * i: k * CT + 128 * (i + 1)], in_=pt[:, 0:128]),
                    reads=[ptk], writes=[("xnT2", k, i)])

        for u in range(NU):
            for dc in range(16):
                for t3, pcoff in enumerate((0, 16, 32)):
                    pc = pcoff + dc
                    wc = w1c[(u * 48 + dc * 3 + t3) % 2]
                    wck = ("w1c", (u * 48 + dc * 3 + t3) % 2)
                    p.dma("gpsimd", wc[:].rearrange("p (k c) -> p k c", k=8),
                          w1_in[:, ts(pc, 128)].rearrange("(k p) c -> p k c", p=128),
                          writes=[wck])
                    for half in range(2):
                        pm = pp[half % 2]
                        pmk = ("pp", half % 2)
                        for k in range(8):
                            p.step("tensor", lambda e, pm=pm, wc=wc, u=u, k=k, half=half: e.matmul(
                                out=pm[:], lhsT=wc[:, ts(k, 128)],
                                rhs=xnT2[:, k * CT + u * UT + half * 512: k * CT + u * UT + (half + 1) * 512],
                                start=(k == 0), stop=(k == 7)),
                                reads=[wck] + [("xnT2", k, i) for i in range(u * 8, u * 8 + 8)],
                                writes=[pmk])
                        p.step("scalar", lambda e, pm=pm, t3=t3, pc=pc, half=half: e.activation(
                            out=p3[t3][:, ts(half, 512)], in_=pm[:], func=AF.Identity,
                            bias=b1c[:, pc:pc + 1]),
                            reads=[pmk, ("b1c", pc)], writes=[("p3", t3, half)])
                p3keys = [("p3", t3, half) for t3 in range(3) for half in range(2)]
                hk, fk_, ik = [("p3", 0, h) for h in range(2)], [("p3", 1, h) for h in range(2)], [("p3", 2, h) for h in range(2)]
                p.step("vector", lambda e: e.tensor_scalar(out=ew["den"][:], in0=p3[1][:],
                                                           scalar1=-80.0, scalar2=None, op0=ALU.max),
                       reads=fk_, writes=["ewden"])
                p.step("scalar", lambda e: e.activation(out=ew["ef"][:], in_=ew["den"][:],
                                                        func=AF.Exp, scale=-1.0),
                       reads=["ewden"], writes=["ewef"])
                p.step("vector", lambda e: e.tensor_scalar(out=ew["den"][:], in0=p3[2][:],
                                                           scalar1=-80.0, scalar2=None, op0=ALU.max),
                       reads=ik, writes=["ewden"])
                p.step("scalar", lambda e: e.activation(out=ew["ei"][:], in_=ew["den"][:],
                                                        func=AF.Exp, scale=-1.0),
                       reads=["ewden"], writes=["ewei"])
                p.step("vector", lambda e: e.tensor_tensor(out=ew["den"][:], in0=ew["ef"][:],
                                                           in1=ew["ei"][:], op=ALU.add),
                       reads=["ewef", "ewei"], writes=["ewden"])
                p.step("vector", lambda e: e.tensor_scalar(out=ew["den"][:], in0=ew["den"][:],
                                                           scalar1=2.0, scalar2=None, op0=ALU.add),
                       reads=["ewden"], writes=["ewden"])
                p.step("vector", lambda e: e.reciprocal(out=ew["den"][:], in_=ew["den"][:]),
                       reads=["ewden"], writes=["ewden"])
                p.step("vector", lambda e: e.tensor_scalar(out=ew["c"][:], in0=ew["ei"][:],
                                                           scalar1=1.0, scalar2=None, op0=ALU.add),
                       reads=["ewei"], writes=["ewc"])
                p.step("vector", lambda e: e.tensor_tensor(out=ew["c"][:], in0=ew["c"][:],
                                                           in1=ew["den"][:], op=ALU.mult),
                       reads=["ewc", "ewden"], writes=["ewc"])
                p.step("vector", lambda e: e.tensor_scalar(out=ew["vc"][:], in0=ew["ef"][:],
                                                           scalar1=1.0, scalar2=None, op0=ALU.add),
                       reads=["ewef"], writes=["ewvc"])
                p.step("vector", lambda e: e.tensor_tensor(out=ew["vc"][:], in0=ew["vc"][:],
                                                           in1=ew["den"][:], op=ALU.mult),
                       reads=["ewvc", "ewden"], writes=["ewvc"])
                p.step("scalar", lambda e: e.activation(out=ew["s1"][:], in_=p3[0][:], func=AF.Sigmoid),
                       reads=hk, writes=["ews1"])
                p.step("scalar", lambda e: e.activation(out=ew["r1"][:], in_=p3[0][:], func=AF.Relu),
                       reads=hk, writes=["ewr1"])
                p.step("vector", lambda e: e.tensor_scalar(out=ew["r1"][:], in0=ew["r1"][:],
                                                           scalar1=0.5, scalar2=None, op0=ALU.add),
                       reads=["ewr1"], writes=["ewr1"])
                p.step("vector", lambda e: e.tensor_single_scalar(out=ew["mask"][:], in_=p3[0][:],
                                                                  scalar=0.0, op=ALU.is_ge),
                       reads=hk, writes=["ewmask"])
                p.step("vector", lambda e: e.select(out=ew["g"][:], mask=ew["mask"][:],
                                                    on_true=ew["r1"][:], on_false=ew["s1"][:],
                                                    add_drain=True),
                       reads=["ewmask", "ewr1", "ews1"], writes=["ewg"])
                p.step("vector", lambda e: e.tensor_tensor(out=ew["v"][:], in0=ew["vc"][:],
                                                           in1=ew["g"][:], op=ALU.mult),
                       reads=["ewvc", "ewg"], writes=["ewv"])
                p.step("vector", lambda e: e.tensor_tensor_scan(out=sch[:], data0=ew["c"][:],
                                                                data1=ew["v"][:], initial=0.0,
                                                                op0=ALU.mult, op1=ALU.add),
                       reads=["ewc", "ewv"], writes=["sch"])
                p.step("vector", lambda e: e.tensor_tensor_scan(out=scp[:], data0=ew["c"][:],
                                                                data1=zeros[:], initial=1.0,
                                                                op0=ALU.mult, op1=ALU.add),
                       reads=["ewc", "zeros"], writes=["scp"])
                p.store(sh_o[u, ts(dc, 128), :], sch[:], reads=["sch"])
                p.store(sp_o[u, ts(dc, 128), :], scp[:], reads=["scp"])
        _fz_end(fz, p)
    return nc



def build_v4(fz=None):
    nc, stack, fz = _fz_begin(fz)
    x2 = _fio(fz, nc, "x2r", [CT, D], F32, "ExternalInput")
    shl = _fio(fz, nc, "scanH", [NU, 2048, UT], F32, "ExternalInput")
    spl = _fio(fz, nc, "scanP", [NU, 2048, UT], F32, "ExternalInput")
    bh_in = _fio(fz, nc, "bndH", [16, 128, 16], F32, "ExternalInput")
    bp_in = _fio(fz, nc, "bndP", [16, 128, 16], F32, "ExternalInput")
    ih_in = _fio(fz, nc, "inith", [128, 16], F32, "ExternalInput")
    w2_in = _fio(fz, nc, "W2", [2048, D], F32, "ExternalInput")
    b2_in = _fio(fz, nc, "b2", [1, D], F32, "ExternalInput")
    fw_in = _fio(fz, nc, "fw", [D, D], F32, "ExternalInput")
    fa_in = _fio(fz, nc, "falpha", [1, 1], F32, "ExternalInput")
    fb_in = _fio(fz, nc, "fbias", [1, D], F32, "ExternalInput")
    dec_in = _fio(fz, nc, "declog", [128, 8], F32, "ExternalInput")
    identf_in = _fio(fz, nc, "ident", [128, 128], F32, "ExternalInput")
    x4_o = _fio(fz, nc, "x4r", [CT, D], F32, "ExternalOutput")
    es_o = _fio(fz, nc, "emaS", [NU, D, UT], F32, "ExternalOutput")
    ep_o = _fio(fz, nc, "emaP", [NU, D, UT], F32, "ExternalOutput")

    with stack:
        sb = lambda nm, shp, dt=BF16: stack.enter_context(nc.sbuf_tensor(_nm(fz, nm), shp, dt))
        psw = lambda nm, shp, dt=F32: stack.enter_context(nc.psum_tensor(_nm(fz, nm), shp, dt))
        identb = sb("identb", [128, 128])
        identf = sb("identf", [128, 128], F32)
        fwc = [sb("fwc0", [128, D], F32)]
        awc = sb("awc", [128, D], F32)
        wtb = sb("wtb", [128, D])
        wtT = sb("wtT", [128, 8 * D])
        asums = sb("asums", [128, 9], F32)
        arow = sb("arow", [1, 128], F32)
        asc = sb("asc", [1, 2], F32)
        delta128 = sb("delta128", [128, 1], F32)
        alpha128 = sb("alpha128", [128, 1], F32)
        al = sb("al", [1, 2], F32)
        fbrow = sb("fbrow", [1, D], F32)
        fbsc = sb("fbsc", [1, D], F32)
        ones_row = sb("ones_row", [1, 128], F32)
        bh = sb("bh", [128, 16 * 16], F32)
        bp = sb("bp", [128, 16 * 16], F32)
        ih = sb("ih", [128, 16], F32)
        gih = sb("gih", [128, 16], F32)
        gs1 = sb("gs1", [128, 16], F32)
        gr1 = sb("gr1", [128, 16], F32)
        gmask = stack.enter_context(nc.sbuf_tensor(_nm(fz, "gmask"), [128, 16], mybir.dt.uint8))
        sin_u = sb("sin_u", [128, NU * 16], F32)
        tmp16 = sb("tmp16", [128, 16], F32)
        hbuf = sb("hbuf", [128, 16 * UT])
        shc = [sb("shc0", [128, UT], F32)]
        spc = [sb("spc0", [128, UT], F32)]
        w2c = sb("w2c", [128, 16 * D])
        b2row = sb("b2row", [1, D], F32)
        xblk = [sb(f"xblk{j}", [128, D], F32) for j in range(2)]
        x4blk = [sb(f"x4blk{j}", [128, D], F32) for j in range(2)]
        xnb = [sb(f"xnb{j}", [128, D]) for j in range(2)]
        xnTb = [sb(f"xnTb{j}", [128, D]) for j in range(2)]
        xnT4 = sb("xnT4", [128, 8 * UT], F32)
        xnf32 = [sb(f"xnf32{j}", [128, D], F32) for j in range(2)]
        dec = sb("dec", [128, 8], F32)
        aa = sb("aa", [128, 8], F32)
        oma = sb("oma", [128, 8], F32)
        atile = sb("atile", [128, UT], F32)
        xmt = sb("xmt", [128, UT], F32)
        sce = sb("sce", [128, UT], F32)
        scf = sb("scf", [128, UT], F32)
        zeros = sb("zeros", [128, UT], F32)
        ptp = [psw(f"ptp{j}", [128, 1024], BF16) for j in range(2)]
        ptpf = [psw(f"ptpf{j}", [128, 512]) for j in range(2)]
        pmm = [psw(f"pmm{j}", [128, 512]) for j in range(2)]
        ptf = psw("ptf", [128, 128])
        p = _fz_prog(fz, nc, stack)
        cpool = ConstPool(nc, stack, p, tag=_nm(fz, 'cp'))
        lnsc = ln_scratch(nc, stack, tag=_nm(fz, 'A'), cpool=cpool)
        lnsc2 = ln_scratch(nc, stack, tag=_nm(fz, 'B'), cpool=cpool)

        p.dma("gpsimd", identb[:], identf_in[:], writes=["identb"])
        p.dma("sync", identf[:], identf_in[:], writes=["identf"])
        p.step("vector", lambda e: e.memset(ones_row[:], 1.0), writes=["ones"])
        p.step("vector", lambda e: e.memset(zeros[:], 0.0), writes=["zeros"])
        p.dma("sync", al[0:1, 0:1], fa_in[:], writes=["al0"])
        p.dma("sync", fbrow[:], fb_in[:], writes=["fbrow"])
        p.dma("sync", b2row[:], b2_in[:], writes=["b2row"])
        p.dma("sync", bh[:].rearrange("p (g cc) -> p g cc", g=16), bh_in.rearrange("g p cc -> p g cc"),
              writes=["bh"])
        p.dma("sync", bp[:].rearrange("p (g cc) -> p g cc", g=16), bp_in.rearrange("g p cc -> p g cc"),
              writes=["bp"])
        p.dma("sync", ih[:], ih_in[:], writes=["ih"])
        p.dma("sync", dec[:], dec_in[:], writes=["dec"])
        for dc in range(16):
            p.dma("gpsimd", w2c[:, ts(dc, D)], w2_in[ts(dc, 128), :], writes=[("w2c", dc)])

        for k in range(8):
            fw = fwc[0]
            fk = ("fwc", 0)
            p.dma("sync", fw[:], fw_in[ts(k, 128), :], writes=[fk])
            p.step("vector", lambda e, fw=fw, k=k: e.reduce_sum(
                out=asums[:, k:k + 1], in_=fw[:], axis=AXX, apply_absolute_value=True),
                reads=[fk], writes=[("as", k)])
        p.step("vector", lambda e: e.reduce_sum(out=asums[:, 8:9], in_=asums[:, 0:8], axis=AXX),
               reads=[("as", k) for k in range(8)], writes=["as8"])
        p.step("tensor", lambda e: e.transpose(out=ptf[0:1, 0:128], in_=asums[:, 8:9],
                                               identity=identf[:]),
               reads=["as8", "identf"], writes=["ptf"])
        p.step("vector", lambda e: e.tensor_copy(out=arow[:], in_=ptf[0:1, 0:128]),
               reads=["ptf"], writes=["arow"])
        p.step("vector", lambda e: e.reduce_sum(out=asc[0:1, 0:1], in_=arow[:], axis=AXX),
               reads=["arow"], writes=["asc0"])
        p.step("vector", lambda e: e.tensor_scalar(out=asc[0:1, 1:2], in0=asc[0:1, 0:1],
                                                   scalar1=0.7 / (D * D), scalar2=None, op0=ALU.mult),
               reads=["asc0"], writes=["asc1"])
        p.step("tensor", lambda e: e.matmul(out=ptf[:, 0:1], lhsT=ones_row[:], rhs=asc[0:1, 1:2],
                                            start=True, stop=True),
               reads=["ones", "asc1"], writes=["ptf"])
        p.step("vector", lambda e: e.tensor_copy(out=delta128[:], in_=ptf[:, 0:1]),
               reads=["ptf"], writes=["delta128"])
        p.step("vector", lambda e: e.reciprocal(out=al[0:1, 1:2], in_=al[0:1, 0:1]),
               reads=["al0"], writes=["al1"])
        p.step("vector", lambda e: e.tensor_scalar(out=fbsc[:], in0=fbrow[:], scalar1=al[0:1, 1:2],
                                                   scalar2=None, op0=ALU.mult),
               reads=["fbrow", "al1"], writes=["fbsc"])
        p.step("tensor", lambda e: e.matmul(out=ptf[:, 1:2], lhsT=ones_row[:], rhs=al[0:1, 0:1],
                                            start=True, stop=True),
               reads=["ones", "al0", "delta128"], writes=["ptf"])
        p.step("vector", lambda e: e.tensor_copy(out=alpha128[:], in_=ptf[:, 1:2]),
               reads=["ptf"], writes=["alpha128"])
        for k in range(8):
            fw = fwc[0]
            fk = ("fwc", 0)
            p.dma("sync", fw[:], fw_in[ts(k, 128), :], writes=[fk])
            p.step("scalar", lambda e, fw=fw: e.activation(out=awc[:], in_=fw[:], func=AF.Abs),
                   reads=[fk], writes=["awc"])
            p.step("scalar", lambda e, fw=fw: e.activation(out=fw[:], in_=fw[:], func=AF.Sign),
                   reads=[fk], writes=[fk])
            p.step("vector", lambda e: e.tensor_single_scalar(out=awc[:], in_=awc[:],
                                                              scalar=delta128[:, 0:1], op=ALU.is_gt),
                   reads=["awc", "delta128"], writes=["awc"])
            p.step("vector", lambda e, fw=fw: e.tensor_tensor(out=wtb[:], in0=awc[:], in1=fw[:], op=ALU.mult),
                   reads=["awc", fk], writes=["wtb"])
            for kk in range(8):
                pt = ptp[kk % 2]
                ptk = ("ptp", kk % 2)
                p.step("tensor", lambda e, pt=pt, kk=kk: e.transpose(
                    out=pt[:, 0:128], in_=wtb[:, ts(kk, 128)], identity=identb[:]),
                    reads=["wtb", "identb"], writes=[ptk])
                p.step("vector", lambda e, pt=pt, k=k, kk=kk: e.tensor_copy(
                    out=wtT[:, kk * 1024 + k * 128: kk * 1024 + (k + 1) * 128], in_=pt[:, 0:128]),
                    reads=[ptk], writes=[("wtT", kk, k)])

        p.step("scalar", lambda e: e.activation(out=gs1[:], in_=ih[:], func=AF.Sigmoid),
               reads=["ih"], writes=["gs1"])
        p.step("scalar", lambda e: e.activation(out=gr1[:], in_=ih[:], func=AF.Relu),
               reads=["ih"], writes=["gr1"])
        p.step("vector", lambda e: e.tensor_scalar(out=gr1[:], in0=gr1[:], scalar1=0.5,
                                                   scalar2=None, op0=ALU.add),
               reads=["gr1"], writes=["gr1"])
        p.step("vector", lambda e: e.tensor_single_scalar(out=gmask[:], in_=ih[:], scalar=0.0,
                                                          op=ALU.is_ge),
               reads=["ih"], writes=["gmask"])
        p.step("vector", lambda e: e.select(out=gih[:], mask=gmask[:], on_true=gr1[:],
                                            on_false=gs1[:], add_drain=True),
               reads=["gmask", "gr1", "gs1"], writes=["gih"])
        for u in range(NU):
            p.step("vector", lambda e, u=u: e.tensor_copy(out=sin_u[:, ts(u, 16)], in_=gih[:]),
                   reads=["gih"], writes=[("sin", u)])
            for j in range(8):
                p.step("vector", lambda e, u=u, j=j: e.tensor_tensor(
                    out=tmp16[:], in0=sin_u[:, ts(u, 16)],
                    in1=bp[:].rearrange("p (u j cc) -> p u j cc", u=NU, j=8)[:, u, j, :], op=ALU.mult),
                    reads=[("sin", u), "bp"], writes=["tmp16"])
                p.step("vector", lambda e, u=u, j=j: e.tensor_tensor(
                    out=sin_u[:, ts(u, 16)], in0=tmp16[:],
                    in1=bh[:].rearrange("p (u j cc) -> p u j cc", u=NU, j=8)[:, u, j, :], op=ALU.add),
                    reads=["tmp16", "bh"], writes=[("sin", u)])

        for u in range(NU):
            for dc in range(16):
                sh_t, sp_t = shc[0], spc[0]
                shk, spk = ("shc", 0), ("spc", 0)
                p.dma("sync", sh_t[:], shl[u, ts(dc, 128), :], writes=[shk])
                p.dma("sync", sp_t[:], spl[u, ts(dc, 128), :], writes=[spk])
                p.step("vector", lambda e, sh_t=sh_t, sp_t=sp_t, u=u, dc=dc: e.scalar_tensor_tensor(
                    out=hbuf[:, ts(dc, UT)], in0=sp_t[:], scalar=sin_u[:, u * 16 + dc: u * 16 + dc + 1],
                    in1=sh_t[:], op0=ALU.mult, op1=ALU.add),
                    reads=[shk, spk, ("sin", u)], writes=[("hbuf", dc)])
            for tb in range(8):
                gi = u * 8 + tb
                xb = x3b = xblk[tb % 2]
                xbk = ("xblk", tb % 2)
                x3k = ("x3blk", tb % 2)
                p.dma("sync", xb[:], x2[ts(gi, 128), :], writes=[xbk])
                for half in range(2):
                    pm = pmm[half % 2]
                    pmk = ("pmm", half % 2)
                    p.step("tensor", lambda e, pm=pm, half=half: e.matmul(
                        out=pm[:], lhsT=ones_row[:], rhs=b2row[0:1, ts(half, 512)],
                        start=True, stop=False),
                        reads=["ones", "b2row"], writes=[pmk])
                    for dc in range(16):
                        p.step("tensor", lambda e, pm=pm, dc=dc, tb=tb, half=half: e.matmul(
                            out=pm[:], lhsT=hbuf[:, dc * UT + tb * 128: dc * UT + (tb + 1) * 128],
                            rhs=w2c[:, dc * D + half * 512: dc * D + (half + 1) * 512],
                            start=False, stop=(dc == 15)),
                            reads=[("hbuf", dc), ("w2c", dc)], writes=[pmk])
                    p.step("vector", lambda e, pm=pm, x3b=x3b, xb=xb, half=half: e.tensor_tensor(
                        out=x3b[:, ts(half, 512)], in0=pm[:], in1=xb[:, ts(half, 512)], op=ALU.add),
                        reads=[pmk, xbk], writes=[(x3k, half)])
                xn, xnT_b, x4b = xnb[tb % 2], xnTb[tb % 2], x4blk[tb % 2]
                xnk, xnTk, x4k = ("xnb", tb % 2), ("xnTb", tb % 2), ("x4blk", tb % 2)
                emit_ln(p, x3b[:], [(x3k, 0), (x3k, 1)], xn[:], xnk, lnsc)
                for k in range(8):
                    pt = ptp[k % 2]
                    ptk = ("ptp", k % 2)
                    p.step("tensor", lambda e, pt=pt, xn=xn, k=k: e.transpose(
                        out=pt[:, 0:128], in_=xn[:, ts(k, 128)], identity=identb[:]),
                        reads=[xnk, "identb"], writes=[ptk])
                    p.step("vector", lambda e, pt=pt, xnT_b=xnT_b, k=k: e.tensor_copy(
                        out=xnT_b[:, ts(k, 128)], in_=pt[:, 0:128]),
                        reads=[ptk], writes=[(xnTk, k)])
                for half in range(2):
                    pm = pmm[half % 2]
                    pmk = ("pmm", half % 2)
                    p.step("tensor", lambda e, pm=pm, half=half: e.matmul(
                        out=pm[:], lhsT=ones_row[:], rhs=fbsc[0:1, ts(half, 512)],
                        start=True, stop=False),
                        reads=["ones", "fbsc"], writes=[pmk])
                    for k in range(8):
                        p.step("tensor", lambda e, pm=pm, xnT_b=xnT_b, k=k, half=half: e.matmul(
                            out=pm[:], lhsT=xnT_b[:, ts(k, 128)],
                            rhs=wtT[:, k * 1024 + half * 512: k * 1024 + (half + 1) * 512],
                            start=False, stop=(k == 7)),
                            reads=[(xnTk, k)] + [("wtT", k, kk) for kk in range(8)], writes=[pmk])
                    p.step("vector", lambda e, pm=pm, x4b=x4b, x3b=x3b, half=half: e.scalar_tensor_tensor(
                        out=x4b[:, ts(half, 512)], in0=pm[:], scalar=alpha128[:, 0:1],
                        in1=x3b[:, ts(half, 512)], op0=ALU.mult, op1=ALU.add),
                        reads=[pmk, (x3k, half), "alpha128"], writes=[(x4k, half)])
                p.store(x4_o[ts(gi, 128), :], x4b[:], reads=[(x4k, 0), (x4k, 1)])
                xnf = xnf32[tb % 2]
                xnfk = ("xnf32", tb % 2)
                emit_ln(p, x4b[:], [(x4k, 0), (x4k, 1)], xn[:], xnk, lnsc2,
                        out2=xnf[:], o2key=xnfk)
                for k in range(8):
                    ptq = ptpf[k % 2]
                    ptqk = ("ptpf", k % 2)
                    p.step("tensor", lambda e, ptq=ptq, xnf=xnf, k=k: e.transpose(
                        out=ptq[:, 0:128], in_=xnf[:, ts(k, 128)], identity=identf[:]),
                        reads=[xnfk, "identf"], writes=[ptqk])
                    p.step("vector", lambda e, ptq=ptq, tb=tb, k=k: e.tensor_copy(
                        out=xnT4[:, k * UT + tb * 128: k * UT + (tb + 1) * 128], in_=ptq[:, 0:128]),
                        reads=[ptqk], writes=[("xnT4", k, tb)])
            if u == 0:
                p.step("scalar", lambda e: e.activation(out=aa[:], in_=dec[:], func=AF.Sigmoid),
                       reads=["dec"], writes=["aa"])
                p.step("vector", lambda e: e.scalar_tensor_tensor(out=oma[:], in0=aa[:], scalar=-1.0,
                                                                  in1=aa[:], op0=ALU.mult,
                                                                  op1=ALU.bypass),
                       reads=["aa"], writes=["oma"])
                p.step("vector", lambda e: e.tensor_scalar(out=oma[:], in0=oma[:], scalar1=1.0,
                                                           scalar2=None, op0=ALU.add),
                       reads=["oma"], writes=["oma"])
            for cc in range(8):
                p.step("scalar", lambda e, cc=cc: e.activation(out=atile[:], in_=zeros[:],
                                                               func=AF.Identity,
                                                               bias=aa[:, cc:cc + 1]),
                       reads=["zeros", "aa"], writes=["atile"])
                p.step("vector", lambda e, cc=cc: e.tensor_single_scalar(
                    out=xmt[:], in_=xnT4[:, ts(cc, UT)], scalar=oma[:, cc:cc + 1], op=ALU.mult),
                    reads=[("xnT4", cc, tb_) for tb_ in range(8)] + ["oma"], writes=["xmt"])
                p.step("vector", lambda e: e.tensor_tensor_scan(out=sce[:], data0=atile[:],
                                                                data1=xmt[:], initial=0.0,
                                                                op0=ALU.mult, op1=ALU.add),
                       reads=["atile", "xmt"], writes=["sce"])
                p.step("vector", lambda e: e.tensor_tensor_scan(out=scf[:], data0=atile[:],
                                                                data1=zeros[:], initial=1.0,
                                                                op0=ALU.mult, op1=ALU.add),
                       reads=["atile", "zeros"], writes=["scf"])
                p.store(es_o[u, ts(cc, 128), :], sce[:], reads=["sce"])
                p.store(ep_o[u, ts(cc, 128), :], scf[:], reads=["scf"])
        _fz_end(fz, p)
    return nc



def build_v5(fz=None):
    nc, stack, fz = _fz_begin(fz)
    x4 = _fio(fz, nc, "x4r", [CT, D], F32, "ExternalInput")
    es_in = _fio(fz, nc, "emaS", [NU, D, UT], F32, "ExternalInput")
    ep_in = _fio(fz, nc, "emaP", [NU, D, UT], F32, "ExternalInput")
    bs_in = _fio(fz, nc, "bndS", [16, 128, 8], F32, "ExternalInput")
    bp_in = _fio(fz, nc, "bndPp", [16, 128, 8], F32, "ExternalInput")
    w_in = _fio(fz, nc, "W", [D, D], F32, "ExternalInput")
    b_in = _fio(fz, nc, "b", [1, D], F32, "ExternalInput")
    cw_in = _fio(fz, nc, "convw", [D, 5], F32, "ExternalInput")
    cb_in = _fio(fz, nc, "convb", [D, 1], F32, "ExternalInput")
    halo_in = _fio(fz, nc, "halo", [NU, 4, D], F32, "ExternalInput")
    identf_in = _fio(fz, nc, "ident", [128, 128], F32, "ExternalInput")
    x5_o = _fio(fz, nc, "x5r", [CT, D], F32, "ExternalOutput")

    with stack:
        sb = lambda nm, shp, dt=BF16: stack.enter_context(nc.sbuf_tensor(_nm(fz, nm), shp, dt))
        psw = lambda nm, shp, dt=F32: stack.enter_context(nc.psum_tensor(_nm(fz, nm), shp, dt))
        identb = sb("identb", [128, 128])
        identf = sb("identf", [128, 128], F32)
        wraw = [sb("wraw0", [128, D])]
        wT = sb("wT", [128, 8 * D])
        brow = sb("brow", [1, D], F32)
        cwb = sb("cwb", [128, 8 * 5], F32)
        cbb = sb("cbb", [128, 8], F32)
        bs = sb("bs", [128, NU * 8 * 8], F32)
        bpp = sb("bpp", [128, NU * 8 * 8], F32)
        sin_u = sb("sin_u", [128, NU * 8], F32)
        tmp8 = sb("tmp8", [128, 8], F32)
        esb = sb("esb", [128, UT], F32)
        epb = sb("epb", [128, UT], F32)
        STb = sb("STb", [128, 8 * UT])
        xnTe = sb("xnTe", [128, 8 * 1028], F32)
        convT = sb("convT", [128, 8 * UT], F32)
        acc = [sb(f"acc{j}", [128, UT], F32) for j in range(2)]
        xblk = [sb(f"xblk{j}", [128, D], F32) for j in range(2)]
        xnb = [sb(f"xnb{j}", [128, D]) for j in range(2)]
        xnf32 = [sb(f"xnf32{j}", [128, D], F32) for j in range(2)]
        hblk = sb("hblk", [4, D], F32)
        hn32 = sb("hn32", [4, D], F32)
        hnjunk = sb("hnjunk", [4, D])
        ylocb = sb("ylocb", [128, D], F32)
        x5b = [sb(f"x5b{j}", [128, D], F32) for j in range(2)]
        ones_row = sb("ones_row", [1, 128], F32)
        ptp = [psw(f"ptp{j}", [128, 1024], BF16) for j in range(2)]
        ptpf = [psw(f"ptpf{j}", [128, 512]) for j in range(2)]
        pmm = [psw(f"pmm{j}", [128, 512]) for j in range(2)]
        p = _fz_prog(fz, nc, stack)
        cpool = ConstPool(nc, stack, p, tag=_nm(fz, 'cp'))
        lnsc = ln_scratch(nc, stack, tag=_nm(fz, 'A'), cpool=cpool)
        lnschalo = ln_scratch(nc, stack, tag=_nm(fz, 'H'), cpool=cpool, rows=4)

        p.dma("gpsimd", identb[:], identf_in[:], writes=["identb"])
        p.dma("sync", identf[:], identf_in[:], writes=["identf"])
        p.step("vector", lambda e: e.memset(ones_row[:], 1.0), writes=["ones"])
        p.dma("sync", brow[:], b_in[:], writes=["brow"])
        p.dma("sync", bs[:].rearrange("p (g j) -> p g j", g=16), bs_in.rearrange("g p j -> p g j"),
              writes=["bs"])
        p.dma("sync", bpp[:].rearrange("p (g j) -> p g j", g=16), bp_in.rearrange("g p j -> p g j"),
              writes=["bpp"])
        for cc in range(8):
            p.dma("sync", cwb[:, ts(cc, 5)], cw_in[ts(cc, 128), :], writes=[("cwb", cc)])
            p.dma("sync", cbb[:, cc:cc + 1], cb_in[ts(cc, 128), :], writes=[("cbb", cc)])
        for k in range(8):
            fw = wraw[0]
            fk = ("wraw", 0)
            p.dma("gpsimd", fw[:], w_in[ts(k, 128), :], writes=[fk])
            for kk in range(8):
                pt = ptp[kk % 2]
                ptk = ("ptp", kk % 2)
                p.step("tensor", lambda e, pt=pt, fw=fw, kk=kk: e.transpose(
                    out=pt[:, 0:128], in_=fw[:, ts(kk, 128)], identity=identb[:]),
                    reads=[fk, "identb"], writes=[ptk])
                p.step("vector", lambda e, pt=pt, k=k, kk=kk: e.tensor_copy(
                    out=wT[:, kk * 1024 + k * 128: kk * 1024 + (k + 1) * 128], in_=pt[:, 0:128]),
                    reads=[ptk], writes=[("wT", kk, k)])

        for u in range(NU):
            p.step("vector", lambda e, u=u: e.memset(sin_u[:, ts(u, 8)], 0.0), writes=[("sin", u)])
            for j in range(8):
                p.step("vector", lambda e, u=u, j=j: e.tensor_tensor(
                    out=tmp8[:], in0=sin_u[:, ts(u, 8)],
                    in1=bpp[:].rearrange("p (u j c) -> p u j c", u=NU, j=8)[:, u, j, :], op=ALU.mult),
                    reads=[("sin", u), "bpp"], writes=["tmp8"])
                p.step("vector", lambda e, u=u, j=j: e.tensor_tensor(
                    out=sin_u[:, ts(u, 8)], in0=tmp8[:],
                    in1=bs[:].rearrange("p (u j c) -> p u j c", u=NU, j=8)[:, u, j, :], op=ALU.add),
                    reads=["tmp8", "bs"], writes=[("sin", u)])

        for u in range(NU):
            for cc in range(8):
                p.dma("sync", esb[:], es_in[u, ts(cc, 128), :], writes=["esb"])
                p.dma("sync", epb[:], ep_in[u, ts(cc, 128), :], writes=["epb"])
                p.step("vector", lambda e, u=u, cc=cc: e.scalar_tensor_tensor(
                    out=STb[:, ts(cc, UT)], in0=epb[:], scalar=sin_u[:, u * 8 + cc: u * 8 + cc + 1],
                    in1=esb[:], op0=ALU.mult, op1=ALU.add),
                    reads=["esb", "epb", ("sin", u)], writes=[("STb", cc)])
            p.dma("sync", hblk[0:4, :], halo_in[u], writes=["hblk"])
            emit_ln(p, hblk[0:4, :], "hblk", hnjunk[0:4, :], "hnjunk", lnschalo,
                    out2=hn32[0:4, :], o2key="hn32")
            for cc in range(8):
                ptq = ptpf[cc % 2]
                ptqk = ("ptpf", cc % 2)
                p.step("tensor", lambda e, ptq=ptq, cc=cc: e.transpose(
                    out=ptq[:, 0:4], in_=hn32[0:4, ts(cc, 128)], identity=identf[0:4, 0:4]),
                    reads=["hn32", "identf"], writes=[ptqk])
                p.step("vector", lambda e, ptq=ptq, cc=cc: e.tensor_copy(
                    out=xnTe[:].rearrange("p (c t) -> p c t", t=1028)[:, cc, 0:2], in_=ptq[:, 0:2]),
                    reads=[ptqk], writes=[("xnTeL", cc)])
                p.step("vector", lambda e, ptq=ptq, cc=cc: e.tensor_copy(
                    out=xnTe[:].rearrange("p (c t) -> p c t", t=1028)[:, cc, 1026:1028], in_=ptq[:, 2:4]),
                    reads=[ptqk], writes=[("xnTeR", cc)])
            for tb in range(8):
                gi = u * 8 + tb
                xb, xn, xnf = xblk[tb % 2], xnb[tb % 2], xnf32[tb % 2]
                xbk, xnk, xnfk = ("xblk", tb % 2), ("xnb", tb % 2), ("xnf32", tb % 2)
                p.dma("sync", xb[:], x4[ts(gi, 128), :], writes=[xbk])
                emit_ln(p, xb[:], xbk, xn[:], xnk, lnsc, out2=xnf[:], o2key=xnfk)
                for k in range(8):
                    ptq = ptpf[k % 2]
                    ptqk = ("ptpf", k % 2)
                    p.step("tensor", lambda e, ptq=ptq, xnf=xnf, k=k: e.transpose(
                        out=ptq[:, 0:128], in_=xnf[:, ts(k, 128)], identity=identf[:]),
                        reads=[xnfk, "identf"], writes=[ptqk])
                    p.step("vector", lambda e, ptq=ptq, tb=tb, k=k: e.tensor_copy(
                        out=xnTe[:].rearrange("p (c t) -> p c t", t=1028)[:, k, 2 + tb * 128: 2 + (tb + 1) * 128],
                        in_=ptq[:, 0:128]),
                        reads=[ptqk], writes=[("xnTe", k, tb)])
            for cc in range(8):
                rdk = [("xnTe", cc, tb_) for tb_ in range(8)] + [("xnTeL", cc), ("xnTeR", cc)]
                for j in range(5):
                    src = acc[(j + 1) % 2]
                    dst = acc[j % 2]
                    if j == 0:
                        p.step("vector", lambda e, cc=cc, dst=dst: e.tensor_single_scalar(
                            out=dst[:], in_=xnTe[:].rearrange("p (c t) -> p c t", t=1028)[:, cc, 0:1024],
                            scalar=cwb[:, cc * 5: cc * 5 + 1], op=ALU.mult),
                            reads=rdk + [("cwb", cc)], writes=[("acc", 0)])
                    else:
                        p.step("vector", lambda e, cc=cc, j=j, src=src, dst=dst: e.scalar_tensor_tensor(
                            out=dst[:], in0=xnTe[:].rearrange("p (c t) -> p c t", t=1028)[:, cc, j:j + 1024],
                            scalar=cwb[:, cc * 5 + j: cc * 5 + j + 1], in1=src[:],
                            op0=ALU.mult, op1=ALU.add),
                            reads=rdk + [("cwb", cc), ("acc", (j + 1) % 2)], writes=[("acc", j % 2)])
                p.step("scalar", lambda e, cc=cc: e.activation(
                    out=convT[:, ts(cc, UT)], in_=acc[0][:], func=AF.Identity,
                    bias=cbb[:, cc:cc + 1]),
                    reads=[("acc", 0), ("cbb", cc)], writes=[("convT", cc)])
            for tb in range(8):
                gi = u * 8 + tb
                xb = xblk[tb % 2]
                xbk = ("xblk", tb % 2)
                p.dma("sync", xb[:], x4[ts(gi, 128), :], writes=[xbk])
                for cc in range(8):
                    ptq = ptpf[cc % 2]
                    ptqk = ("ptpf", cc % 2)
                    p.step("tensor", lambda e, ptq=ptq, cc=cc, tb=tb: e.transpose(
                        out=ptq[:, 0:128], in_=convT[:, cc * UT + tb * 128: cc * UT + (tb + 1) * 128],
                        identity=identf[:]),
                        reads=[("convT", cc), "identf"], writes=[ptqk])
                    p.step("vector", lambda e, ptq=ptq, cc=cc: e.tensor_copy(
                        out=ylocb[:, ts(cc, 128)], in_=ptq[:, 0:128]),
                        reads=[ptqk], writes=[("ylocb", cc)])
                x5t = x5b[tb % 2]
                x5k = ("x5b", tb % 2)
                for half in range(2):
                    pm = pmm[half % 2]
                    pmk = ("pmm", half % 2)
                    p.step("tensor", lambda e, pm=pm, half=half: e.matmul(
                        out=pm[:], lhsT=ones_row[:], rhs=brow[0:1, ts(half, 512)],
                        start=True, stop=False),
                        reads=["ones", "brow"], writes=[pmk])
                    for cc in range(8):
                        p.step("tensor", lambda e, pm=pm, cc=cc, tb=tb, half=half: e.matmul(
                            out=pm[:], lhsT=STb[:, cc * UT + tb * 128: cc * UT + (tb + 1) * 128],
                            rhs=wT[:, cc * 1024 + half * 512: cc * 1024 + (half + 1) * 512],
                            start=False, stop=(cc == 7)),
                            reads=[("STb", cc)] + [("wT", cc, k) for k in range(8)], writes=[pmk])
                    p.step("vector", lambda e, pm=pm, x5t=x5t, xb=xb, half=half: e.tensor_tensor(
                        out=x5t[:, ts(half, 512)], in0=pm[:], in1=xb[:, ts(half, 512)], op=ALU.add),
                        reads=[pmk, xbk], writes=[(x5k, half)])
                    p.step("vector", lambda e, x5t=x5t, half=half: e.tensor_tensor(
                        out=x5t[:, ts(half, 512)], in0=x5t[:, ts(half, 512)],
                        in1=ylocb[:, ts(half, 512)], op=ALU.add),
                        reads=[(x5k, half)] + [("ylocb", cc) for cc in range(4 * half, 4 * half + 4)],
                        writes=[(x5k, half)])
                p.store(x5_o[ts(gi, 128), :], x5t[:], reads=[(x5k, 0), (x5k, 1)])
        _fz_end(fz, p)
    return nc



def build_v6(fz=None):
    nc, stack, fz = _fz_begin(fz)
    xall = _fio(fz, nc, "xall", [NU, 2050, D], F32, "ExternalInput")
    wqkv_in = _fio(fz, nc, "wqkv", [D, 1536], F32, "ExternalInput")
    wo_in = _fio(fz, nc, "wo", [512, D], F32, "ExternalInput")
    lg_in = _fio(fz, nc, "lng", [1, D], F32, "ExternalInput")
    lb_in = _fio(fz, nc, "lnb", [1, D], F32, "ExternalInput")
    w1_in = _fio(fz, nc, "w1", [D, 5460], F32, "ExternalInput")
    b1_in = _fio(fz, nc, "b1", [5460, 1], F32, "ExternalInput")
    w2_in = _fio(fz, nc, "w2", [2730, D], F32, "ExternalInput")
    b2_in = _fio(fz, nc, "b2", [1, D], F32, "ExternalInput")
    qs_in = _fio(fz, nc, "qsrow", [1, 512], F32, "ExternalInput")
    ks_in = _fio(fz, nc, "ksrow", [1, 512], F32, "ExternalInput")
    csq_in = _fio(fz, nc, "csq", [NU, 2048, 512], F32, "ExternalInput")
    ssq_in = _fio(fz, nc, "ssq", [NU, 2048, 512], F32, "ExternalInput")
    csk_in = _fio(fz, nc, "csk", [NU, 2048, 512], F32, "ExternalInput")
    ssk_in = _fio(fz, nc, "ssk", [NU, 2048, 512], F32, "ExternalInput")
    wm_in = _fio(fz, nc, "wmask", [1, NU * 4], F32, "ExternalInput")
    identf_in = _fio(fz, nc, "ident", [128, 128], F32, "ExternalInput")
    xo_o = _fio(fz, nc, "xor", [CT, D], F32, "ExternalOutput")
    xmid_h = _fio(fz, nc, "xmidh", [NU, 12 * 128, D], F32, "Internal")

    HID = 2730
    NHC = 22

    def hc_rows(hc):
        return 42 if hc == 21 else 128

    with stack:
        sb = lambda nm, shp, dt=BF16: stack.enter_context(nc.sbuf_tensor(_nm(fz, nm), shp, dt))
        psw = lambda nm, shp, dt=F32: stack.enter_context(nc.psum_tensor(_nm(fz, nm), shp, dt))
        identb = sb("identb", [128, 128])
        arena1 = sb("arena1", [128, 40960])
        w2c = [sb(f"w2c{j}", [128, 1024]) for j in range(2)]
        h3T = sb("h3T", [128, 8 * 1024])
        wqkvb = sb("wqkvb", [128, 8 * 1536])
        wob = sb("wob", [128, 4 * D])
        w1c = [sb("w1ca", [128, 8 * 128]), sb("w1cg", [128, 8 * 128])]
        b1c = sb("b1c", [128, 2 * NHC], F32)
        b2row = sb("b2row", [1, D], F32)
        lgrow = sb("lgrow", [1, D], F32)
        lbrow = sb("lbrow", [1, D], F32)
        lgb = sb("lgb", [128, D])
        lbb = sb("lbb", [128, D])
        qsb = sb("qsb", [128, 512], F32)
        ksb = sb("ksb", [128, 512], F32)
        wmrow = sb("wmrow", [1, 512], F32)
        wmr = sb("wmr", [1, NU * 4], F32)
        ones_row = sb("ones_row", [1, 512], F32)
        hb = [sb("hb0", [128, D])] * 2
        hTb = [sb("hTb0", [128, 8 * 128])] * 2
        qkb = [sb("qkb0", [128, 512], F32)] * 2
        rot = sb("rot", [128, 512], F32)
        cstt = [sb("cst0", [128, 512], F32)] * 2
        sstt = [sb("sst0", [128, 512], F32)] * 2
        nrm = sb("nrm", [128, 8], F32)
        srow = sb("srow", [128, 1024], F32)
        t1 = srow
        aTs = srow
        gTs_off = 512
        negm = sb("negm", [128, 1], F32)
        lsum = sb("lsum", [128, 1], F32)
        rl = sb("rl", [128, 1], F32)
        Pb = sb("Pb", [128, 1024])
        dmask = sb("dmask", [128, 8 * 512])
        PTb = sb("PTb", [128, 8 * 128])
        ob = sb("ob", [128, 512])
        oTb = sb("oTb", [128, 4 * 128])
        xmb = [sb("xmb0", [128, D], F32)] * 2
        h2b = [sb("h2b0", [128, D], F32)] * 2
        xnb = [sb("xnb0", [128, D])] * 2
        h3b = [sb("h3b0", [128, D])] * 2
        xob = [sb("xob0", [128, D], F32)] * 2
        psA = psw("psA", [128, 1024])
        psB = [psw(f"psB{j}", [128, 512]) for j in range(2)]
        psC = [psw(f"psC{j}", [128, 512]) for j in range(2)]
        ptp = [psw(f"ptp{j}", [128, 1024], BF16) for j in range(2)]
        p = _fz_prog(fz, nc, stack)
        cpool = ConstPool(nc, stack, p, tag=_nm(fz, 'cp'))
        lnsc = ln_scratch(nc, stack, tag=_nm(fz, 'A'), cpool=cpool, sq_tile=Pb, sq_key="Pb")

        KTOFF, VOFF = 16384, 32768

        p.dma("gpsimd", identb[:], identf_in[:], writes=["identb"])
        p.step("vector", lambda e: e.memset(ones_row[:], 1.0), writes=["ones"])
        for k in range(8):
            p.dma("gpsimd", wqkvb[:, ts(k, 1536)], wqkv_in[ts(k, 128), :], writes=[("wqkvb", k)])
        for k in range(4):
            p.dma("gpsimd", wob[:, ts(k, D)], wo_in[ts(k, 128), :], writes=[("wob", k)])
        p.dma("sync", b2row[:], b2_in[:], writes=["b2row"])
        p.dma("sync", lgrow[:], lg_in[:], writes=["lgrow"])
        p.dma("sync", lbrow[:], lb_in[:], writes=["lbrow"])
        p.dma("sync", wmr[:], wm_in[:], writes=["wmr"])
        for nm, src, dst, rk in (("r_qs", qs_in, qsb, "qsb"), ("r_ks", ks_in, ksb, "ksb")):
            p.dma("sync", t1[0:1, 0:512], src[:], writes=[nm, "srow"])
            p.step("tensor", lambda e: e.matmul(out=psB[0][:], lhsT=ones_row[0:1, 0:128],
                                                rhs=t1[0:1, 0:512], start=True, stop=True),
                   reads=[nm, "srow", "ones"], writes=[("psB", 0)])
            p.step("vector", lambda e, dst=dst: e.tensor_copy(out=dst[:], in_=psB[0][:]),
                   reads=[("psB", 0)], writes=[rk])
        for nm, row, dst, rk in (("lgrow", lgrow, lgb, "lgb"), ("lbrow", lbrow, lbb, "lbb")):
            for half in range(2):
                p.step("tensor", lambda e, row=row, half=half: e.matmul(
                    out=psB[half % 2][:], lhsT=ones_row[0:1, 0:128],
                    rhs=row[0:1, ts(half, 512)], start=True, stop=True),
                    reads=[nm, "ones"], writes=[("psB", half % 2)])
                p.step("vector", lambda e, dst=dst, half=half: e.tensor_copy(
                    out=dst[:, ts(half, 512)], in_=psB[half % 2][:]),
                    reads=[("psB", half % 2)], writes=[(rk, half)])
        for qb in range(4):
            p.step("gpsimd", lambda e, qb=qb: e.iota(
                out=dmask[:, ts(qb, 512)], pattern=[[-1, 512]], base=(128 * qb) if qb else None,
                channel_multiplier=1, allow_small_or_imprecise_dtypes=True),
                writes=[("dmask", qb)])
            p.step("gpsimd", lambda e, qb=qb: e.iota(
                out=dmask[:, ts(4 + qb, 512)], pattern=[[-1, 512]], base=(128 * qb) if qb else None,
                channel_multiplier=1, allow_small_or_imprecise_dtypes=True),
                writes=[("dmask", 4 + qb)])
            p.step("vector", lambda e, qb=qb: e.tensor_single_scalar(
                out=dmask[:, ts(qb, 512)], in_=dmask[:, ts(qb, 512)], scalar=0.0, op=ALU.is_lt),
                reads=[("dmask", qb)], writes=[("dmask", qb)])
            p.step("vector", lambda e, qb=qb: e.tensor_scalar(
                out=dmask[:, ts(qb, 512)], in0=dmask[:, ts(qb, 512)], scalar1=-30000.0,
                scalar2=None, op0=ALU.mult),
                reads=[("dmask", qb)], writes=[("dmask", qb)])
            p.step("vector", lambda e, qb=qb: e.tensor_single_scalar(
                out=dmask[:, ts(4 + qb, 512)], in_=dmask[:, ts(4 + qb, 512)], scalar=0.0,
                op=ALU.is_gt),
                reads=[("dmask", 4 + qb)], writes=[("dmask", 4 + qb)])
            p.step("vector", lambda e, qb=qb: e.tensor_scalar(
                out=dmask[:, ts(4 + qb, 512)], in0=dmask[:, ts(4 + qb, 512)], scalar1=-30000.0,
                scalar2=None, op0=ALU.mult),
                reads=[("dmask", 4 + qb)], writes=[("dmask", 4 + qb)])
        for hc in range(NHC):
            r = hc_rows(hc)
            p.dma("sync", b1c[0:r, hc:hc + 1], b1_in[hc * 128: hc * 128 + r, :], writes=[("b1a", hc)])
            p.dma("sync", b1c[0:r, NHC + hc:NHC + hc + 1],
                  b1_in[HID + hc * 128: HID + hc * 128 + r, :], writes=[("b1g", hc)])

        for u in range(NU):
            for blk in range(16):
                hbt, hTt = hb[blk % 2], hTb[blk % 2]
                hbk0, hbk1, hTk = ("hb", 0, 0), ("hb", 0, 1), ("hT", 0)
                r0 = 2 + blk * 128
                p.dma("gpsimd", hbt[:, 0:512], xall[u, r0:r0 + 128, 0:512], writes=[hbk0])
                p.dma("gpsimd", hbt[:, 512:1024], xall[u, r0 - 1:r0 + 127, 512:1024], writes=[hbk1])
                for k in range(8):
                    pt = ptp[k % 2]
                    ptk = ("ptp", k % 2)
                    p.step("tensor", lambda e, pt=pt, hbt=hbt, k=k: e.transpose(
                        out=pt[:, 0:128], in_=hbt[:, ts(k, 128)], identity=identb[:]),
                        reads=[hbk0, hbk1, "identb"], writes=[ptk])
                    p.step("vector", lambda e, pt=pt, hTt=hTt, k=k: e.tensor_copy(
                        out=hTt[:, ts(k, 128)], in_=pt[:, 0:128]),
                        reads=[ptk], writes=[(hTk, k)])
                pm = psB[blk % 2]
                pmk = ("psB", blk % 2)
                for k in range(8):
                    p.step("tensor", lambda e, pm=pm, hTt=hTt, k=k: e.matmul(
                        out=pm[:], lhsT=hTt[:, ts(k, 128)],
                        rhs=wqkvb[:, k * 1536 + 1024:k * 1536 + 1536],
                        start=(k == 0), stop=(k == 7)),
                        reads=[(hTk, k), ("wqkvb", k)], writes=[pmk])
                p.step("vector", lambda e, pm=pm, blk=blk: e.tensor_copy(
                    out=arena1[:, VOFF + blk * 512: VOFF + (blk + 1) * 512], in_=pm[:]),
                    reads=[pmk], writes=[("v", blk)])
                for which, coff, sclk, cin, sin_ in (("q", 0, "qsb", csq_in, ssq_in),
                                                     ("k", 512, "ksb", csk_in, ssk_in)):
                    qk = qkb[blk % 2]
                    qkk = ("qkb", 0, which)
                    pm2 = psC[blk % 2]
                    pm2k = ("psC", blk % 2)
                    for k in range(8):
                        p.step("tensor", lambda e, pm2=pm2, hTt=hTt, k=k, coff=coff: e.matmul(
                            out=pm2[:], lhsT=hTt[:, ts(k, 128)],
                            rhs=wqkvb[:, k * 1536 + coff:k * 1536 + coff + 512],
                            start=(k == 0), stop=(k == 7)),
                            reads=[(hTk, k), ("wqkvb", k)], writes=[pm2k])
                    p.step("vector", lambda e, qk=qk, pm2=pm2: e.tensor_copy(out=qk[:], in_=pm2[:]),
                           reads=[pm2k], writes=[qkk])
                    p.step("vector", lambda e, qk=qk: e.tensor_tensor(out=t1[:, 0:512], in0=qk[:],
                                                                      in1=qk[:], op=ALU.mult),
                           reads=[qkk], writes=["srow"])
                    for h in range(8):
                        p.step("vector", lambda e, h=h: e.reduce_sum(
                            out=nrm[:, h:h + 1], in_=t1[:, ts(h, 64)], axis=AXX),
                            reads=["srow"], writes=[("nrm", h)])
                    p.step("scalar", lambda e: e.activation(out=nrm[:], in_=nrm[:], func=AF.Sqrt),
                           reads=[("nrm", h) for h in range(8)], writes=["nrmall"])
                    p.step("vector", lambda e: e.tensor_scalar(out=nrm[:], in0=nrm[:], scalar1=1e-12,
                                                               scalar2=None, op0=ALU.max),
                           reads=["nrmall"], writes=["nrmall"])
                    p.step("vector", lambda e: e.reciprocal(out=nrm[:], in_=nrm[:]),
                           reads=["nrmall"], writes=["nrmall"])
                    for h in range(8):
                        p.step("vector", lambda e, qk=qk, h=h: e.tensor_single_scalar(
                            out=qk[:, ts(h, 64)], in_=qk[:, ts(h, 64)], scalar=nrm[:, h:h + 1],
                            op=ALU.mult),
                            reads=[qkk, "nrmall"], writes=[qkk])
                    p.step("vector", lambda e, qk=qk, sclk=sclk: e.tensor_tensor(
                        out=qk[:], in0=qk[:], in1=(qsb if sclk == "qsb" else ksb)[:], op=ALU.mult),
                        reads=[qkk, sclk], writes=[qkk])
                    ct, st_ = cstt[blk % 2], sstt[blk % 2]
                    ctk, stk = ("cst", 0), ("sst", 0)
                    p.dma("sync", ct[:], cin[u, ts(blk, 128), :], writes=[ctk])
                    p.dma("sync", st_[:], sin_[u, ts(blk, 128), :], writes=[stk])
                    p.step("vector", lambda e, qk=qk: e.tensor_copy(
                        out=rot[:].rearrange("p (h a d) -> p h a d", h=8, a=2)[:, :, 0, :],
                        in_=qk[:].rearrange("p (h a d) -> p h a d", h=8, a=2)[:, :, 1, :]),
                        reads=[qkk], writes=["rot0"])
                    p.step("vector", lambda e, qk=qk: e.tensor_copy(
                        out=rot[:].rearrange("p (h a d) -> p h a d", h=8, a=2)[:, :, 1, :],
                        in_=qk[:].rearrange("p (h a d) -> p h a d", h=8, a=2)[:, :, 0, :]),
                        reads=[qkk], writes=["rot1"])
                    p.step("vector", lambda e, qk=qk, ct=ct: e.tensor_tensor(
                        out=qk[:], in0=qk[:], in1=ct[:], op=ALU.mult),
                        reads=[qkk, ctk], writes=[qkk])
                    p.step("vector", lambda e, st_=st_: e.tensor_tensor(
                        out=rot[:], in0=rot[:], in1=st_[:], op=ALU.mult),
                        reads=["rot0", "rot1", stk], writes=["rotm"])
                    p.step("vector", lambda e, qk=qk: e.tensor_tensor(
                        out=qk[:], in0=qk[:], in1=rot[:], op=ALU.add),
                        reads=[qkk, "rotm"], writes=[qkk])
                    p.step("vector", lambda e, qk=qk: e.tensor_copy(out=Pb[:, 0:512], in_=qk[:]),
                           reads=[qkk], writes=["Pbq"])
                    coloff = 0 if which == "q" else KTOFF
                    for h in range(8):
                        pt = ptp[h % 2]
                        ptk2 = ("ptp", h % 2)
                        p.step("tensor", lambda e, pt=pt, h=h: e.transpose(
                            out=pt[0:64, 0:128], in_=Pb[:, ts(h, 64)], identity=identb[:]),
                            reads=["Pbq", "identb"], writes=[ptk2])
                        p.step("vector", lambda e, pt=pt, h=h, blk=blk, coloff=coloff,
                               which=which: e.tensor_copy(
                            out=arena1[0:64,
                                       coloff + h * 2048 + blk * 128:coloff + h * 2048 + (blk + 1) * 128],
                            in_=pt[0:64, 0:128]),
                            reads=[ptk2], writes=[(which + "T", h, blk)])
            for w4 in range(1, 4):
                p.step("vector", lambda e, u=u, w4=w4: e.tensor_scalar(
                    out=wmrow[0:1, :], in0=ones_row[0:1, 0:512],
                    scalar1=wmr[0:1, u * 4 + w4:u * 4 + w4 + 1], scalar2=None, op0=ALU.mult),
                    reads=["ones", "wmr"], writes=["wmrow"])
                for qb in range(4):
                    blk = w4 * 4 + qb
                    for h in range(8):
                        p.step("tensor", lambda e: e.matmul(
                            out=psA[:, 0:512], lhsT=ones_row[0:1, 0:128],
                            rhs=wmrow[0:1, :], start=True, stop=False),
                            reads=["wmrow", "ones"], writes=[("psA", 0)])
                        p.step("tensor", lambda e, h=h, w4=w4, blk=blk: e.matmul(
                            out=psA[:, 0:512],
                            lhsT=arena1[0:64, h * 2048 + blk * 128: h * 2048 + (blk + 1) * 128],
                            rhs=arena1[0:64, KTOFF + h * 2048 + (w4 - 1) * 512: KTOFF + h * 2048 + w4 * 512],
                            start=False, stop=True),
                            reads=[("qT", h, blk)] + [("kT", h, bb_) for bb_ in range(4 * (w4 - 1), 4 * w4)],
                            writes=[("psA", 0)])
                        p.step("tensor", lambda e, h=h, w4=w4, blk=blk: e.matmul(
                            out=psA[:, 512:1024],
                            lhsT=arena1[0:64, h * 2048 + blk * 128: h * 2048 + (blk + 1) * 128],
                            rhs=arena1[0:64, KTOFF + h * 2048 + w4 * 512: KTOFF + h * 2048 + (w4 + 1) * 512],
                            start=True, stop=True),
                            reads=[("qT", h, blk)] + [("kT", h, bb_) for bb_ in range(4 * w4, 4 * w4 + 4)],
                            writes=[("psA", 1)])
                        p.step("vector", lambda e: e.tensor_scalar(
                            out=srow[:], in0=psA[:], scalar1=8.0, scalar2=None, op0=ALU.mult),
                            reads=[("psA", 0), ("psA", 1)], writes=["srow"])
                        p.step("vector", lambda e, qb=qb: e.tensor_tensor(
                            out=srow[:, 512:1024], in0=srow[:, 512:1024],
                            in1=dmask[:, ts(qb, 512)], op=ALU.add),
                            reads=["srow", ("dmask", qb)], writes=["srow"])
                        p.step("vector", lambda e, qb=qb: e.tensor_tensor(
                            out=srow[:, 0:512], in0=srow[:, 0:512],
                            in1=dmask[:, ts(4 + qb, 512)], op=ALU.add),
                            reads=["srow", ("dmask", 4 + qb)], writes=["srow"])
                        p.step("vector", lambda e: e.reduce_max(out=negm[:], in_=srow[:], axis=AXX,
                                                                negate=True),
                               reads=["srow"], writes=["negm"])
                        p.step("scalar", lambda e: e.activation(out=Pb[:], in_=srow[:], func=AF.Exp,
                                                                bias=negm[:, 0:1], accum_out=lsum[:]),
                               reads=["srow", "negm"], writes=["Pb", "lsum"])
                        p.step("vector", lambda e: e.reciprocal(out=rl[:], in_=lsum[:]),
                               reads=["lsum"], writes=["rl"])
                        p.step("vector", lambda e: e.tensor_single_scalar(
                            out=Pb[:], in_=Pb[:], scalar=rl[:, 0:1], op=ALU.mult),
                            reads=["Pb", "rl"], writes=["Pb"])
                        for kk in range(8):
                            pt = ptp[kk % 2]
                            ptk = ("ptp", kk % 2)
                            p.step("tensor", lambda e, pt=pt, kk=kk: e.transpose(
                                out=pt[:, 0:128], in_=Pb[:, ts(kk, 128)], identity=identb[:]),
                                reads=["Pb", "identb"], writes=[ptk])
                            p.step("vector", lambda e, pt=pt, kk=kk: e.tensor_copy(
                                out=PTb[:, ts(kk, 128)], in_=pt[:, 0:128]),
                                reads=[ptk], writes=[("PTb", kk)])
                        for kk in range(8):
                            kvb = 4 * (w4 - 1) + kk
                            p.step("tensor", lambda e, h=h, kk=kk, kvb=kvb: e.matmul(
                                out=psB[0][:, h * 64:(h + 1) * 64],
                                lhsT=PTb[:, ts(kk, 128)],
                                rhs=arena1[:, VOFF + kvb * 512 + h * 64: VOFF + kvb * 512 + (h + 1) * 64],
                                start=(h == 0 and kk == 0), stop=(h == 7 and kk == 7)),
                                reads=[("PTb", kk), ("v", kvb)], writes=[("psB", 0)])
                    p.step("vector", lambda e: e.tensor_copy(out=ob[:], in_=psB[0][:]),
                           reads=[("psB", 0)], writes=["ob"])
                    for kk in range(4):
                        pt = ptp[kk % 2]
                        ptk = ("ptp", kk % 2)
                        p.step("tensor", lambda e, pt=pt, kk=kk: e.transpose(
                            out=pt[:, 0:128], in_=ob[:, ts(kk, 128)], identity=identb[:]),
                            reads=["ob", "identb"], writes=[ptk])
                        p.step("vector", lambda e, pt=pt, kk=kk: e.tensor_copy(
                            out=oTb[:, ts(kk, 128)], in_=pt[:, 0:128]),
                            reads=[ptk], writes=[("oTb", kk)])
                    xm = xmb[qb % 2]
                    xmk = ("xmb", 0)
                    r0 = 2 + blk * 128
                    p.dma("sync", xm[:], xall[u, r0:r0 + 128, :], writes=[xmk])
                    xo = xob[qb % 2]
                    xok = ("xob", 0)
                    for half in range(2):
                        pm = psC[half % 2]
                        pmk = ("psC", half % 2)
                        for kk in range(4):
                            p.step("tensor", lambda e, pm=pm, kk=kk, half=half: e.matmul(
                                out=pm[:], lhsT=oTb[:, ts(kk, 128)],
                                rhs=wob[:, kk * D + half * 512: kk * D + half * 512 + 512],
                                start=(kk == 0), stop=(kk == 3)),
                                reads=[("oTb", kk), ("wob", kk)], writes=[pmk])
                        p.step("vector", lambda e, pm=pm, xo=xo, xm=xm, half=half: e.tensor_tensor(
                            out=xo[:, ts(half, 512)], in0=pm[:], in1=xm[:, ts(half, 512)], op=ALU.add),
                            reads=[pmk, xmk], writes=[(xok, half)])
                    p.dma("sync", xmid_h[u, ts((w4 - 1) * 4 + qb, 128), :], xo[:],
                          reads=[(xok, 0), (xok, 1)], writes=[("xmidh", u, (w4 - 1) * 4 + qb)])
            for tb8 in range(8):
                blk12 = 4 + tb8
                h2 = h2b[tb8 % 2]
                h2k0, h2k1 = ("h2b", 0, 0), ("h2b", 0, 1)
                r0 = blk12 * 128
                p.dma("gpsimd", h2[:, 0:512], xmid_h[u, r0:r0 + 128, 0:512],
                      reads=[("xmidh", u, blk12)], writes=[h2k0])
                p.dma("gpsimd", h2[:, 512:1024], xmid_h[u, r0 - 1:r0 + 127, 512:1024],
                      reads=[("xmidh", u, blk12), ("xmidh", u, blk12 - 1)], writes=[h2k1])
                xn = xnb[0]
                xnk = ("xnb", 0)
                emit_ln(p, h2[:], [h2k0, h2k1], xn[:], xnk, lnsc)
                h3 = h3b[0]
                h3k = ("h3b", 0)
                for half in range(2):
                    p.step("vector", lambda e, xn=xn, half=half: e.tensor_tensor(
                        out=srow[:, ts(half, 512)], in0=xn[:, ts(half, 512)],
                        in1=lgb[:, ts(half, 512)], op=ALU.mult),
                        reads=[xnk, ("lgb", half)], writes=["srow"])
                    p.step("vector", lambda e, h3=h3, half=half: e.tensor_tensor(
                        out=h3[:, ts(half, 512)], in0=srow[:, ts(half, 512)],
                        in1=lbb[:, ts(half, 512)], op=ALU.add),
                        reads=["srow", ("lbb", half)], writes=[(h3k, half)])
                for k in range(8):
                    pt = ptp[k % 2]
                    ptk = ("ptp", k % 2)
                    p.step("tensor", lambda e, pt=pt, h3=h3, k=k: e.transpose(
                        out=pt[:, 0:128], in_=h3[:, ts(k, 128)], identity=identb[:]),
                        reads=[(h3k, 0), (h3k, 1), "identb"], writes=[ptk])
                    p.step("vector", lambda e, pt=pt, tb8=tb8, k=k: e.tensor_copy(
                        out=h3T[:, k * 1024 + tb8 * 128: k * 1024 + (tb8 + 1) * 128],
                        in_=pt[:, 0:128]),
                        reads=[ptk], writes=[("h3T", k, tb8)])
            for hc in range(NHC):
                r = hc_rows(hc)
                wca, wcg = w1c[0], w1c[1]
                p.dma("gpsimd", wca[:].rearrange("p (k c) -> p k c", k=8)[:, :, 0:r],
                      w1_in[:, hc * 128: hc * 128 + r].rearrange("(k p) c -> p k c", p=128),
                      writes=[("w1a",)])
                p.dma("gpsimd", wcg[:].rearrange("p (k c) -> p k c", k=8)[:, :, 0:r],
                      w1_in[:, HID + hc * 128: HID + hc * 128 + r].rearrange("(k p) c -> p k c", p=128),
                      writes=[("w1g",)])
                for half in range(2):
                    for src, pm, pmk, bk, bcol in ((0, psB[0], ("psB", 0), ("b1a", hc), hc),
                                                   (1, psB[1], ("psB", 1), ("b1g", hc), NHC + hc)):
                        wc = wca if src == 0 else wcg
                        wck = ("w1a",) if src == 0 else ("w1g",)
                        for k in range(8):
                            p.step("tensor", lambda e, pm=pm, wc=wc, k=k, r=r, half=half: e.matmul(
                                out=pm[0:r, :], lhsT=wc[:, k * 128: k * 128 + r],
                                rhs=h3T[:, k * 1024 + half * 512: k * 1024 + half * 512 + 512],
                                start=(k == 0), stop=(k == 7)),
                                reads=[wck] + [("h3T", k, t8) for t8 in range(4 * half, 4 * half + 4)],
                                writes=[pmk])
                        if src == 0:
                            p.step("scalar", lambda e, pm=pm, r=r, half=half, bcol=bcol: e.activation(
                                out=srow[0:r, 0:512], in_=pm[0:r, :], func=AF.Identity,
                                bias=b1c[0:r, bcol:bcol + 1]),
                                reads=[pmk, bk], writes=["srow"])
                        else:
                            p.step("scalar", lambda e, pm=pm, r=r, half=half, bcol=bcol: e.activation(
                                out=srow[0:r, 512:1024], in_=pm[0:r, :], func=AF.Gelu,
                                bias=b1c[0:r, bcol:bcol + 1]),
                                reads=[pmk, bk], writes=["srow"])
                    p.step("vector", lambda e, r=r, hc=hc, half=half: e.tensor_tensor(
                        out=arena1[0:r, hc * 1024 + half * 512: hc * 1024 + half * 512 + 512],
                        in0=srow[0:r, 0:512], in1=srow[0:r, 512:1024], op=ALU.mult),
                        reads=["srow"], writes=[("gd", hc, half)])
            for pr in range(4):
                pms = [psB[0], psB[1], psC[0], psC[1]]
                pmks = [("psB", 0), ("psB", 1), ("psC", 0), ("psC", 1)]
                for t2 in range(2):
                    for half in range(2):
                        j = t2 * 2 + half
                        p.step("tensor", lambda e, j=j, half=half, pms=pms: e.matmul(
                            out=pms[j][:], lhsT=ones_row[0:1, 0:128], rhs=b2row[0:1, ts(half, 512)],
                            start=True, stop=False),
                            reads=["b2row", "ones"], writes=[pmks[j]])
                for hc in range(NHC):
                    r = hc_rows(hc)
                    wc = w2c[hc % 2]
                    wck = ("w2s", hc % 2)
                    p.dma("gpsimd", wc[0:r, :], w2_in[hc * 128:hc * 128 + r, :], writes=[wck])
                    for t2 in range(2):
                        tb8 = pr * 2 + t2
                        for half in range(2):
                            j = t2 * 2 + half
                            p.step("tensor", lambda e, j=j, hc=hc, r=r, tb8=tb8, half=half,
                                   wc=wc, pms=pms: e.matmul(
                                out=pms[j][:],
                                lhsT=arena1[0:r, hc * 1024 + tb8 * 128: hc * 1024 + (tb8 + 1) * 128],
                                rhs=wc[0:r, ts(half, 512)],
                                start=False, stop=(hc == NHC - 1)),
                                reads=[("gd", hc, 0), ("gd", hc, 1), wck], writes=[pmks[j]])
                for t2 in range(2):
                    tb8 = pr * 2 + t2
                    blk12 = 4 + tb8
                    xm = xmb[0]
                    xmk = ("xmb", 0)
                    p.dma("sync", xm[:], xmid_h[u, ts(blk12, 128), :],
                          reads=[("xmidh", u, blk12)], writes=[xmk])
                    xo = xob[0]
                    xok = ("xob", 0)
                    for half in range(2):
                        j = t2 * 2 + half
                        p.step("vector", lambda e, j=j, xo=xo, xm=xm, half=half, pms=pms: e.tensor_tensor(
                            out=xo[:, ts(half, 512)], in0=pms[j][:], in1=xm[:, ts(half, 512)], op=ALU.add),
                            reads=[pmks[j], xmk], writes=[(xok, half)])
                    p.store(xo_o[ts(u * 8 + tb8, 128), :], xo[:], reads=[(xok, 0), (xok, 1)])
        _fz_end(fz, p)
    return nc


def host_v6_maps(X, l, inp):
    DH, WINDOW = 64, 512
    n = T
    half_idx = np.arange(0, DH, 2, dtype=np.float64)
    inv_freq = 1.0 / 10000.0 ** (half_idx / DH)
    tpos = np.arange(n, dtype=np.float64)
    freqs = tpos[:, None] * inv_freq[None, :]
    freqs = np.concatenate([freqs, freqs], axis=-1)
    base = (half_idx + 0.4 * DH) / (1.4 * DH)
    power = (tpos - n // 2) / float(WINDOW // 2)
    scale = base[None, :] ** power[:, None]
    scale = np.concatenate([scale, scale], axis=-1)
    cos, sin = np.cos(freqs), np.sin(freqs)
    sgn = np.concatenate([-np.ones(DH // 2), np.ones(DH // 2)])
    csq_t = (cos * scale).astype(np.float32)
    ssq_t = (sin * scale * sgn).astype(np.float32)
    csk_t = (cos / scale).astype(np.float32)
    ssk_t = (sin / scale * sgn).astype(np.float32)

    def tile8(tab):
        return np.tile(tab, (1, 8))

    maps = []
    for i in range(NCORE):
        us = units_for_core(i)
        xalls = np.zeros((NU, 2050, D), np.float32)
        csq = np.zeros((NU, 2048, 512), np.float32)
        ssq = np.zeros((NU, 2048, 512), np.float32)
        csk = np.zeros((NU, 2048, 512), np.float32)
        ssk = np.zeros((NU, 2048, 512), np.float32)
        wm = np.zeros((1, NU * 4), np.float32)
        for ui, (b, c) in enumerate(us):
            s = 1024 * c
            lo = s - 1026
            xs = X[b, max(lo, 0):s + 1024]
            xalls[ui, 2050 - xs.shape[0]:] = xs
            poslo = s - 1024
            for w4 in range(4):
                gw = (s // 512) - 2 + w4
                wm[0, ui * 4 + w4] = -30000.0 if gw == 0 else 0.0
            pidx = np.arange(poslo, s + 1024)
            pidx_c = np.clip(pidx, 0, n - 1)
            csq[ui] = tile8(csq_t[pidx_c])
            ssq[ui] = tile8(ssq_t[pidx_c])
            csk[ui] = tile8(csk_t[pidx_c])
            ssk[ui] = tile8(ssk_t[pidx_c])
        maps.append({
            "xall": xalls,
            "wqkv": np.asarray(inp["L1_ffn_wqkv"][l]),
            "wo": np.asarray(inp["L1_ffn_wo"][l]),
            "lng": np.asarray(inp["L1_ffn_ln_g"][l])[None, :],
            "lnb": np.asarray(inp["L1_ffn_ln_b"][l])[None, :],
            "w1": np.asarray(inp["L1_ffn_w1"][l]),
            "b1": np.asarray(inp["L1_ffn_b1"][l])[:, None],
            "w2": np.asarray(inp["L1_ffn_w2"][l]),
            "b2": np.asarray(inp["L1_ffn_b2"][l])[None, :],
            "qsrow": np.tile(np.asarray(inp["L1_ffn_q_scale"][l]), 8)[None, :],
            "ksrow": np.tile(np.asarray(inp["L1_ffn_k_scale"][l]), 8)[None, :],
            "csq": csq, "ssq": ssq, "csk": csk, "ssk": ssk,
            "wmask": wm, "ident": _IDENT,
        })
    return maps



def build_v7(fz=None):
    fused = fz is not None
    nc, stack, fz = _fz_begin(fz)
    if fused:
        win = _fio(fz, nc, "Wsig", [D, D], F32, "ExternalInput")
        wout = None
        f_o = _fio(fz, nc, "fsigloc", [1, 1], F32, "Internal")
    else:
        win = _fio(fz, nc, "Win", [D, D], F32, "ExternalInput")
        wout = _fio(fz, nc, "Wout", [D, D], F32, "ExternalInput")
        f_o = _fio(fz, nc, "fsig", [1, 2], F32, "ExternalOutput")
    identf_in = _fio(fz, nc, "ident", [128, 128], F32, "ExternalInput")
    NSQ, NPI = 6, 32

    with stack:
        sb = lambda nm, shp, dt=F32: stack.enter_context(nc.sbuf_tensor(_nm(fz, nm), shp, dt))
        psw = lambda nm, shp, dt=F32: stack.enter_context(nc.psum_tensor(_nm(fz, nm), shp, dt))
        identf = sb("identf", [128, 128])
        wbuf = sb("wbuf", [128, 8 * D])
        M = sb("M", [128, 8 * D])
        M2 = sb("M2", [128, 8 * D])
        vv = sb("vv", [128, 8])
        vn = sb("vn", [128, 8])
        mx8 = sb("mx8", [128, 9])
        mrow = sb("mrow", [1, 128])
        msc = sb("msc", [1, 8])
        b128 = sb("b128", [128, 2])
        ones_row = sb("ones_row", [1, 128])
        fout = sb("fout", [1, 2])
        pm = [psw(f"pm{j}", [128, 512]) for j in range(2)]
        ptf = psw("ptf", [128, 128])
        p = _fz_prog(fz, nc, stack)
        cpool = ConstPool(nc, stack, p, tag=_nm(fz, 'cp'))

        p.dma("sync", identf[:], identf_in[:], writes=["identf"])
        p.step("vector", lambda e: e.memset(ones_row[:], 1.0), writes=["ones"])

        def part_reduce(srckey, dstcol, op):
            p.step("tensor", lambda e: e.transpose(out=ptf[0:1, 0:128], in_=mx8[:, 8:9],
                                                   identity=identf[:]),
                   reads=[srckey, "identf"], writes=["ptf"])
            p.step("vector", lambda e: e.tensor_copy(out=mrow[:], in_=ptf[0:1, 0:128]),
                   reads=["ptf"], writes=["mrow"])
            p.step("vector", lambda e, dstcol=dstcol, op=op: e.tensor_reduce(
                out=msc[0:1, dstcol:dstcol + 1], in_=mrow[:], axis=AXX, op=op),
                reads=["mrow"], writes=[("msc", dstcol)])

        wmats = (win,) if fused else (win, wout)
        for wi, w_in in enumerate(wmats):
            for k in range(8):
                p.dma("sync", wbuf[:, ts(k, D)], w_in[ts(k, 128), :], writes=[("wb", k)])
            for m in range(8):
                for half in range(2):
                    pmt = pm[half % 2]
                    pmk = ("pm", half % 2)
                    for k in range(8):
                        p.step("tensor", lambda e, pmt=pmt, m=m, k=k, half=half: e.matmul(
                            out=pmt[:], lhsT=wbuf[:, k * D + m * 128: k * D + (m + 1) * 128],
                            rhs=wbuf[:, k * D + half * 512: k * D + (half + 1) * 512],
                            start=(k == 0), stop=(k == 7)),
                            reads=[("wb", k)], writes=[pmk]) 
                    p.step("scalar", lambda e, pmt=pmt, m=m, half=half: e.activation(
                        out=M[:, m * D + half * 512: m * D + (half + 1) * 512], in_=pmt[:],
                        func=AF.Identity, scale=0.25),
                        reads=[pmk], writes=[("M", m, half)])
            p.step("vector", lambda e: e.memset(msc[0:1, 1:2], 0.0), writes=[("msc", 1)])
            Mk = [("M", m, h) for m in range(8) for h in range(2)]
            for sq in range(NSQ):
                for m in range(8):
                    p.step("vector", lambda e, m=m: e.reduce_max(
                        out=mx8[:, m:m + 1], in_=M[:, ts(m, D)], axis=AXX,
                        apply_absolute_value=True),
                        reads=Mk, writes=[("mx", m)])
                p.step("vector", lambda e: e.tensor_reduce(
                    out=mx8[:, 8:9], in_=mx8[:, 0:8], axis=AXX, op=ALU.max),
                    reads=[("mx", m) for m in range(8)], writes=["mxall"])
                part_reduce("mxall", 0, ALU.max)
                p.step("scalar", lambda e: e.activation(out=msc[0:1, 5:6], in_=msc[0:1, 0:1],
                                                        func=AF.Ln),
                       reads=[("msc", 0)], writes=[("msc", 5)])
                p.step("vector", lambda e, sq=sq: e.scalar_tensor_tensor(
                    out=msc[0:1, 1:2], in0=msc[0:1, 5:6], scalar=float(2 ** (NSQ - sq)),
                    in1=msc[0:1, 1:2], op0=ALU.mult, op1=ALU.add),
                    reads=[("msc", 5), ("msc", 1)], writes=[("msc", 1)])
                p.step("vector", lambda e: e.reciprocal(out=msc[0:1, 6:7], in_=msc[0:1, 0:1]),
                       reads=[("msc", 0)], writes=[("msc", 6)])
                p.step("tensor", lambda e: e.matmul(out=ptf[:, 0:1], lhsT=ones_row[:],
                                                    rhs=msc[0:1, 6:7], start=True, stop=True),
                       reads=[("msc", 6), "ones"], writes=["ptf"])
                p.step("vector", lambda e: e.tensor_copy(out=b128[:, 0:1], in_=ptf[:, 0:1]),
                       reads=["ptf"], writes=["b128"])
                for m in range(8):
                    p.step("vector", lambda e, m=m: e.tensor_single_scalar(
                        out=M[:, ts(m, D)], in_=M[:, ts(m, D)], scalar=b128[:, 0:1], op=ALU.mult),
                        reads=Mk + ["b128"], writes=[("M", m, 0), ("M", m, 1)])
                for m in range(8):
                    for half in range(2):
                        pmt = pm[half % 2]
                        pmk = ("pm", half % 2)
                        for k in range(8):
                            p.step("tensor", lambda e, pmt=pmt, m=m, k=k, half=half: e.matmul(
                                out=pmt[:], lhsT=M[:, k * D + m * 128: k * D + (m + 1) * 128],
                                rhs=M[:, k * D + half * 512: k * D + (half + 1) * 512],
                                start=(k == 0), stop=(k == 7)),
                                reads=Mk, writes=[pmk])
                        p.step("vector", lambda e, pmt=pmt, m=m, half=half: e.tensor_copy(
                            out=M2[:, m * D + half * 512: m * D + (half + 1) * 512], in_=pmt[:]),
                            reads=[pmk], writes=[("M2", m, half)])
                for m in range(8):
                    p.step("vector", lambda e, m=m: e.tensor_copy(out=M[:, ts(m, D)],
                                                                  in_=M2[:, ts(m, D)]),
                           reads=[("M2", m, 0), ("M2", m, 1)], writes=[("M", m, 0), ("M", m, 1)])
            p.step("gpsimd", lambda e: e.iota(out=vv[:], pattern=[[1, 8]], base=None,
                                              channel_multiplier=8,
                                              allow_small_or_imprecise_dtypes=True),
                   writes=["vv"])
            p.step("vector", lambda e: e.tensor_scalar(out=vv[:], in0=vv[:], scalar1=0.001,
                                                       scalar2=1.0, op0=ALU.mult, op1=ALU.add),
                   reads=["vv"], writes=["vv"])
            for it in range(NPI):
                for m in range(8):
                    pmt = pm[m % 2]
                    pmk = ("pm", m % 2)
                    for k in range(8):
                        p.step("tensor", lambda e, pmt=pmt, m=m, k=k: e.matmul(
                            out=pmt[:, 0:1], lhsT=M[:, k * D + m * 128: k * D + (m + 1) * 128],
                            rhs=vv[:, k:k + 1], start=(k == 0), stop=(k == 7)),
                            reads=Mk + ["vv"], writes=[pmk])
                    p.step("vector", lambda e, pmt=pmt, m=m: e.tensor_copy(
                        out=vn[:, m:m + 1], in_=pmt[:, 0:1]),
                        reads=[pmk], writes=[("vn", m)])
                vnk = [("vn", m) for m in range(8)]
                if it < NPI - 1:
                    p.step("vector", lambda e: e.reduce_max(
                        out=mx8[:, 8:9], in_=vn[:], axis=AXX,
                        apply_absolute_value=True),
                        reads=vnk, writes=["mxall"])
                    part_reduce("mxall", 2, ALU.max)
                    p.step("vector", lambda e: e.reciprocal(out=msc[0:1, 6:7], in_=msc[0:1, 2:3]),
                           reads=[("msc", 2)], writes=[("msc", 6)])
                    p.step("tensor", lambda e: e.matmul(out=ptf[:, 0:1], lhsT=ones_row[:],
                                                        rhs=msc[0:1, 6:7], start=True, stop=True),
                           reads=[("msc", 6), "ones"], writes=["ptf"])
                    p.step("vector", lambda e: e.tensor_copy(out=b128[:, 0:1], in_=ptf[:, 0:1]),
                           reads=["ptf"], writes=["b128"])
                    p.step("vector", lambda e: e.tensor_single_scalar(
                        out=vv[:], in_=vn[:], scalar=b128[:, 0:1], op=ALU.mult),
                        reads=vnk + ["b128"], writes=["vv"])
            p.step("scalar", lambda e: e.activation(out=mx8[:, 0:8], in_=vv[:], func=AF.Square,
                                                    accum_out=mx8[:, 8:9]),
                   reads=["vv"], writes=["mxsq", "mxall"])
            part_reduce("mxall", 3, ALU.add)
            p.step("scalar", lambda e: e.activation(out=mx8[:, 0:8], in_=vn[:], func=AF.Square,
                                                    accum_out=mx8[:, 8:9]),
                   reads=[("vn", m) for m in range(8)], writes=["mxsq", "mxall"])
            part_reduce("mxall", 4, ALU.add)
            p.step("scalar", lambda e: e.activation(out=msc[0:1, 5:6], in_=msc[0:1, 4:5], func=AF.Ln),
                   reads=[("msc", 4)], writes=[("msc", 5)])
            p.step("scalar", lambda e: e.activation(out=msc[0:1, 6:7], in_=msc[0:1, 3:4], func=AF.Ln),
                   reads=[("msc", 3)], writes=[("msc", 6)])
            p.step("vector", lambda e: e.tensor_tensor(out=msc[0:1, 5:6], in0=msc[0:1, 5:6],
                                                       in1=msc[0:1, 6:7], op=ALU.subtract),
                   reads=[("msc", 5), ("msc", 6)], writes=[("msc", 5)])
            p.step("vector", lambda e: e.tensor_scalar(out=msc[0:1, 5:6], in0=msc[0:1, 5:6],
                                                       scalar1=0.5, scalar2=None, op0=ALU.mult),
                   reads=[("msc", 5)], writes=[("msc", 5)])
            p.step("vector", lambda e: e.tensor_tensor(out=msc[0:1, 5:6], in0=msc[0:1, 5:6],
                                                       in1=msc[0:1, 1:2], op=ALU.add),
                   reads=[("msc", 5), ("msc", 1)], writes=[("msc", 5)])
            p.step("vector", lambda e: e.tensor_scalar(
                out=msc[0:1, 5:6], in0=msc[0:1, 5:6], scalar1=1.0 / (2 ** NSQ),
                scalar2=float(np.log(4.0)), op0=ALU.mult, op1=ALU.add),
                reads=[("msc", 5)], writes=[("msc", 5)])
            p.step("scalar", lambda e: e.activation(out=msc[0:1, 7:8], in_=msc[0:1, 5:6],
                                                    func=AF.Exp, scale=0.5),
                   reads=[("msc", 5)], writes=[("msc", 7)])
            p.step("vector", lambda e: e.tensor_scalar(out=msc[0:1, 7:8], in0=msc[0:1, 7:8],
                                                       scalar1=1.0, scalar2=None, op0=ALU.max),
                   reads=[("msc", 7)], writes=[("msc", 7)])
            p.step("vector", lambda e, wi=wi: e.reciprocal(out=fout[0:1, wi:wi + 1],
                                                           in_=msc[0:1, 7:8]),
                   reads=[("msc", 7)], writes=[("fout", wi)])
        if fused:
            p.store(f_o[:], fout[0:1, 0:1], reads=[("fout", 0)])
        else:
            p.store(f_o[:], fout[0:1, 0:2], reads=[("fout", 0), ("fout", 1)])
        _fz_end(fz, p)
    return nc



def build_v8a(fz=None):
    nc, stack, fz = _fz_begin(fz)
    x6 = _fio(fz, nc, "x6r", [CT, D], F32, "ExternalInput")
    wp_in = _fio(fz, nc, "Wp", [3072, D], F32, "ExternalInput")
    bp_in = _fio(fz, nc, "bp", [3072, 1], F32, "ExternalInput")
    win_in = _fio(fz, nc, "Win", [D, D], F32, "ExternalInput")
    al_in = _fio(fz, nc, "alog", [128, 8], F32, "ExternalInput")
    db_in = _fio(fz, nc, "dbias", [128, 8], F32, "ExternalInput")
    lg_in = _fio(fz, nc, "lgam", [1, 1], F32, "ExternalInput")
    fs_in = _fio(fz, nc, "fsig", [1, 2], F32, "ExternalInput")
    identf_in = _fio(fz, nc, "ident", [128, 128], F32, "ExternalInput")
    sS_o = _fio(fz, nc, "sS", [NU, D, UT], F32, "ExternalOutput")
    sP_o = _fio(fz, nc, "sP", [NU, D, UT], F32, "ExternalOutput")
    cT_o = _fio(fz, nc, "cT", [NU, D, UT], F32, "ExternalOutput")

    with stack:
        sb = lambda nm, shp, dt=BF16: stack.enter_context(nc.sbuf_tensor(_nm(fz, nm), shp, dt))
        psw = lambda nm, shp, dt=F32: stack.enter_context(nc.psum_tensor(_nm(fz, nm), shp, dt))
        identb = sb("identb", [128, 128])
        xblk = [sb(f"xblk{j}", [128, D], F32) for j in range(2)]
        xnb = [sb(f"xnb{j}", [128, D]) for j in range(2)]
        xnT = sb("xnT", [128, 8 * CT])
        wpc = sb("wpc", [128, D], F32)
        wpT = sb("wpT", [128, 8 * 128])
        bpc = sb("bpc", [128, 24], F32)
        alog = sb("alogs", [128, 8], F32)
        dbias = sb("dbiass", [128, 8], F32)
        alpha = sb("alpha", [128, 8], F32)
        spz = sb("spz", [128, 8], F32)
        gs1 = sb("gs1", [1, 4], F32)
        gs128 = sb("gs128", [128, 1], F32)
        ones_row = sb("ones_row", [1, 128], F32)
        p3 = [sb(f"p3_{j}", [128, UT], F32) for j in range(3)]
        uT = sb("uT", [128, UT], F32)
        ew = {nm: stack.enter_context(nc.sbuf_tensor(_nm(fz, f"e{nm}"), [128, UT], F32))
              for nm in ("z", "az", "en", "sp", "a", "b", "c", "pp", "q2", "disc", "scl", "bu")}
        zeros = sb("zeros", [128, UT], F32)
        sch = sb("sch", [128, UT], F32)
        scp = sb("scp", [128, UT], F32)
        ptp = [psw(f"ptp{j}", [128, 1024], BF16) for j in range(2)]
        pp = [psw(f"pp{j}", [128, 512]) for j in range(2)]
        ptf = psw("ptf", [128, 128])
        p = _fz_prog(fz, nc, stack)
        cpool = ConstPool(nc, stack, p, tag=_nm(fz, 'cp'))
        lnsc = ln_scratch(nc, stack, tag=_nm(fz, 'A'), cpool=cpool)
        one_ap, one_key = cpool.get(1.0)

        p.dma("gpsimd", identb[:], identf_in[:], writes=["identb"])
        p.step("vector", lambda e: e.memset(ones_row[:], 1.0), writes=["ones"])
        p.step("vector", lambda e: e.memset(zeros[:], 0.0), writes=["zeros"])
        p.dma("sync", alog[:], al_in[:], writes=["alog"])
        p.dma("sync", dbias[:], db_in[:], writes=["dbias"])
        p.dma("sync", gs1[0:1, 0:1], lg_in[:], writes=[("gs", 0)])
        p.dma("sync", gs1[0:1, 1:3], fs_in[:], writes=[("gs", 1)])
        for kc in range(24):
            p.dma("sync", bpc[:, kc:kc + 1], bp_in[ts(kc, 128), :], writes=[("bpc", kc)])
        p.step("scalar", lambda e: e.activation(out=spz[:], in_=alog[:], func=AF.Abs),
               reads=["alog"], writes=["spz"])
        p.step("scalar", lambda e: e.activation(out=spz[:], in_=spz[:], func=AF.Exp, scale=-1.0),
               reads=["spz"], writes=["spz"])
        p.step("scalar", lambda e: e.activation(out=spz[:], in_=spz[:], func=AF.Ln,
                                                bias=one_ap),
               reads=["spz", one_key], writes=["spz"])
        p.step("scalar", lambda e: e.activation(out=alpha[:], in_=alog[:], func=AF.Relu),
               reads=["alog"], writes=["alpha"])
        p.step("vector", lambda e: e.tensor_tensor(out=alpha[:], in0=alpha[:], in1=spz[:],
                                                   op=ALU.add),
               reads=["alpha", "spz"], writes=["alpha"])
        p.step("scalar", lambda e: e.activation(out=gs1[0:1, 3:4], in_=gs1[0:1, 0:1], func=AF.Exp),
               reads=[("gs", 0)], writes=[("gs", 3)])
        p.step("vector", lambda e: e.tensor_tensor(out=gs1[0:1, 3:4], in0=gs1[0:1, 3:4],
                                                   in1=gs1[0:1, 1:2], op=ALU.mult),
               reads=[("gs", 3), ("gs", 1)], writes=[("gs", 3)])
        p.step("tensor", lambda e: e.matmul(out=ptf[:, 0:1], lhsT=ones_row[:], rhs=gs1[0:1, 3:4],
                                            start=True, stop=True),
               reads=[("gs", 3), "ones"], writes=["ptf"])
        p.step("vector", lambda e: e.tensor_copy(out=gs128[:], in_=ptf[:, 0:1]),
               reads=["ptf"], writes=["gs128"])

        for i in range(NB):
            xb, xn = xblk[i % 2], xnb[i % 2]
            xbk, xnk = ("xblk", i % 2), ("xnb", i % 2)
            p.dma("sync", xb[:], x6[ts(i, 128), :], writes=[xbk])
            emit_ln(p, xb[:], xbk, xn[:], xnk, lnsc)
            for k in range(8):
                pt = ptp[k % 2]
                ptk = ("ptp", k % 2)
                p.step("tensor", lambda e, pt=pt, xn=xn, k=k: e.transpose(
                    out=pt[:, 0:128], in_=xn[:, ts(k, 128)], identity=identb[:]),
                    reads=[xnk, "identb"], writes=[ptk])
                p.step("vector", lambda e, pt=pt, i=i, k=k: e.tensor_copy(
                    out=xnT[:, k * CT + 128 * i: k * CT + 128 * (i + 1)], in_=pt[:, 0:128]),
                    reads=[ptk], writes=[("xnT", k, i)])

        def projT(src_in, rowoff, dst, dstk, biascol, bk, scale_ap, sk, u):
            p.dma("gpsimd", wpc[:], src_in[rowoff:rowoff + 128, :], writes=["wpc"])
            p.step("vector", lambda e: e.tensor_copy(out=xnb[0][:], in_=wpc[:]),
                   reads=["wpc"], writes=[("xnb", 0)])
            for k in range(8):
                pt = ptp[k % 2]
                ptk = ("ptp", k % 2)
                p.step("tensor", lambda e, pt=pt, k=k: e.transpose(
                    out=pt[:, 0:128], in_=xnb[0][:, ts(k, 128)], identity=identb[:]),
                    reads=[("xnb", 0), "identb"], writes=[ptk])
                p.step("vector", lambda e, pt=pt, k=k: e.tensor_copy(
                    out=wpT[:, ts(k, 128)], in_=pt[:, 0:128]),
                    reads=[ptk], writes=[("wpT", k)])
            for half in range(2):
                pm = pp[half % 2]
                pmk = ("pp", half % 2)
                for k in range(8):
                    p.step("tensor", lambda e, pm=pm, k=k, half=half, u=u: e.matmul(
                        out=pm[:], lhsT=wpT[:, ts(k, 128)],
                        rhs=xnT[:, k * CT + u * UT + half * 512: k * CT + u * UT + (half + 1) * 512],
                        start=(k == 0), stop=(k == 7)),
                        reads=[("wpT", k)] + [("xnT", k, u * 8 + j) for j in range(8)],
                        writes=[pmk])
                if scale_ap is None:
                    p.step("scalar", lambda e, pm=pm, half=half: e.activation(
                        out=dst[:, ts(half, 512)], in_=pm[:], func=AF.Identity,
                        bias=biascol),
                        reads=[pmk, bk], writes=[(dstk, half)])
                else:
                    p.step("scalar", lambda e, pm=pm, half=half: e.activation(
                        out=dst[:, ts(half, 512)], in_=pm[:], func=AF.Identity,
                        scale=scale_ap),
                        reads=[pmk, sk], writes=[(dstk, half)])

        for u in range(NU):
            for dc in range(8):
                for t3, base in enumerate((0, 8, 16)):
                    kc = base + dc
                    projT(wp_in, kc * 128, p3[t3], ("p3", t3), bpc[:, kc:kc + 1], ("bpc", kc),
                          None, None, u)
                projT(win_in, dc * 128, uT, "uT", None, None, gs128[:, 0:1], "gs128", u)
                p3k = lambda t3: [("p3", t3, 0), ("p3", t3, 1)]
                p.step("scalar", lambda e, dc=dc: e.activation(out=ew["z"][:], in_=p3[0][:],
                                                               func=AF.Identity,
                                                               bias=dbias[:, dc:dc + 1]),
                       reads=p3k(0) + ["dbias"], writes=["ez"])
                p.step("scalar", lambda e: e.activation(out=ew["az"][:], in_=ew["z"][:], func=AF.Abs),
                       reads=["ez"], writes=["eaz"])
                p.step("scalar", lambda e: e.activation(out=ew["en"][:], in_=ew["az"][:],
                                                        func=AF.Exp, scale=-1.0),
                       reads=["eaz"], writes=["een"])
                p.step("scalar", lambda e: e.activation(out=ew["en"][:], in_=ew["en"][:],
                                                        func=AF.Ln, bias=one_ap),
                       reads=["een", one_key], writes=["een"])
                p.step("scalar", lambda e: e.activation(out=ew["sp"][:], in_=ew["z"][:], func=AF.Relu),
                       reads=["ez"], writes=["esp"])
                p.step("vector", lambda e: e.tensor_tensor(out=ew["sp"][:], in0=ew["sp"][:],
                                                           in1=ew["en"][:], op=ALU.add),
                       reads=["esp", "een"], writes=["esp"])
                p.step("vector", lambda e, dc=dc: e.tensor_single_scalar(
                    out=ew["a"][:], in_=ew["sp"][:], scalar=alpha[:, dc:dc + 1], op=ALU.mult),
                    reads=["esp", "alpha"], writes=["ea"])
                p.step("scalar", lambda e: e.activation(out=ew["a"][:], in_=ew["a"][:],
                                                        func=AF.Exp, scale=-1.0),
                       reads=["ea"], writes=["ea"])
                p.step("vector", lambda e: e.tensor_scalar(out=ew["a"][:], in0=ew["a"][:],
                                                           scalar1=1.0 - 1e-4, scalar2=None,
                                                           op0=ALU.min),
                       reads=["ea"], writes=["ea"])
                p.step("scalar", lambda e: e.activation(out=ew["b"][:], in_=p3[1][:], func=AF.Tanh),
                       reads=p3k(1), writes=["eb"])
                p.step("scalar", lambda e: e.activation(out=ew["c"][:], in_=p3[2][:], func=AF.Tanh),
                       reads=p3k(2), writes=["ec"])
                p.step("vector", lambda e: e.tensor_tensor(out=ew["pp"][:], in0=ew["a"][:],
                                                           in1=ew["a"][:], op=ALU.mult),
                       reads=["ea"], writes=["epp"])
                p.step("vector", lambda e: e.scalar_tensor_tensor(out=ew["disc"][:], in0=ew["c"][:],
                                                                  scalar=1.0, in1=ew["c"][:],
                                                                  op0=ALU.mult, op1=ALU.mult),
                       reads=["ec"], writes=["edisc"])
                p.step("vector", lambda e: e.tensor_tensor(out=ew["pp"][:], in0=ew["pp"][:],
                                                           in1=ew["disc"][:], op=ALU.add),
                       reads=["epp", "edisc"], writes=["epp"])
                p.step("vector", lambda e: e.tensor_tensor(out=ew["q2"][:], in0=ew["b"][:],
                                                           in1=ew["b"][:], op=ALU.mult),
                       reads=["eb"], writes=["eq2"])
                p.step("vector", lambda e: e.tensor_tensor(out=ew["disc"][:], in0=ew["pp"][:],
                                                           in1=ew["q2"][:], op=ALU.subtract),
                       reads=["epp", "eq2"], writes=["edisc"])
                p.step("vector", lambda e: e.tensor_tensor(out=ew["disc"][:], in0=ew["disc"][:],
                                                           in1=ew["disc"][:], op=ALU.mult),
                       reads=["edisc"], writes=["edisc"])
                p.step("vector", lambda e: e.tensor_tensor(out=ew["scl"][:], in0=ew["a"][:],
                                                           in1=ew["b"][:], op=ALU.mult),
                       reads=["ea", "eb"], writes=["escl"])
                p.step("vector", lambda e: e.tensor_tensor(out=ew["scl"][:], in0=ew["scl"][:],
                                                           in1=ew["scl"][:], op=ALU.mult),
                       reads=["escl"], writes=["escl"])
                p.step("vector", lambda e: e.scalar_tensor_tensor(out=ew["disc"][:], in0=ew["scl"][:],
                                                                  scalar=4.0, in1=ew["disc"][:],
                                                                  op0=ALU.mult, op1=ALU.add),
                       reads=["escl", "edisc"], writes=["edisc"])
                p.step("scalar", lambda e: e.activation(out=ew["disc"][:], in_=ew["disc"][:],
                                                        func=AF.Sqrt, bias=cpool.get(1e-12)[0]),
                       reads=["edisc", cpool.get(1e-12)[1]], writes=["edisc"])
                p.step("vector", lambda e: e.tensor_tensor(out=ew["scl"][:], in0=ew["pp"][:],
                                                           in1=ew["q2"][:], op=ALU.add),
                       reads=["epp", "eq2"], writes=["escl"])
                p.step("vector", lambda e: e.tensor_tensor(out=ew["scl"][:], in0=ew["scl"][:],
                                                           in1=ew["disc"][:], op=ALU.add),
                       reads=["escl", "edisc"], writes=["escl"])
                p.step("vector", lambda e: e.tensor_scalar(out=ew["scl"][:], in0=ew["scl"][:],
                                                           scalar1=0.5, scalar2=1e-12,
                                                           op0=ALU.mult, op1=ALU.add),
                       reads=["escl"], writes=["escl"])
                p.step("scalar", lambda e: e.activation(out=ew["scl"][:], in_=ew["scl"][:],
                                                        func=AF.Sqrt),
                       reads=["escl"], writes=["escl"])
                p.step("vector", lambda e: e.tensor_scalar(out=ew["scl"][:], in0=ew["scl"][:],
                                                           scalar1=1.0, scalar2=None, op0=ALU.max),
                       reads=["escl"], writes=["escl"])
                p.step("vector", lambda e: e.reciprocal(out=ew["scl"][:], in_=ew["scl"][:]),
                       reads=["escl"], writes=["escl"])
                p.step("vector", lambda e: e.tensor_tensor(out=ew["a"][:], in0=ew["a"][:],
                                                           in1=ew["scl"][:], op=ALU.mult),
                       reads=["ea", "escl"], writes=["ea"])
                p.step("vector", lambda e: e.tensor_tensor(out=ew["b"][:], in0=ew["b"][:],
                                                           in1=ew["scl"][:], op=ALU.mult),
                       reads=["eb", "escl"], writes=["eb"])
                p.step("vector", lambda e: e.tensor_tensor(out=ew["c"][:], in0=ew["c"][:],
                                                           in1=ew["scl"][:], op=ALU.mult),
                       reads=["ec", "escl"], writes=["ec"])
                p.step("vector", lambda e: e.tensor_tensor(out=ew["bu"][:], in0=ew["b"][:],
                                                           in1=uT[:], op=ALU.mult),
                       reads=["eb", ("uT", 0), ("uT", 1)], writes=["ebu"])
                p.step("vector", lambda e: e.tensor_tensor_scan(out=sch[:], data0=ew["a"][:],
                                                                data1=ew["bu"][:], initial=0.0,
                                                                op0=ALU.mult, op1=ALU.add),
                       reads=["ea", "ebu"], writes=["sch"])
                p.step("vector", lambda e: e.tensor_tensor_scan(out=scp[:], data0=ew["a"][:],
                                                                data1=zeros[:], initial=1.0,
                                                                op0=ALU.mult, op1=ALU.add),
                       reads=["ea", "zeros"], writes=["scp"])
                p.store(sS_o[u, ts(dc, 128), :], sch[:], reads=["sch"])
                p.store(sP_o[u, ts(dc, 128), :], scp[:], reads=["scp"])
                p.store(cT_o[u, ts(dc, 128), :], ew["c"][:], reads=["ec"])
        _fz_end(fz, p)
    return nc


def build_v8b(fz=None):
    nc, stack, fz = _fz_begin(fz)
    x6 = _fio(fz, nc, "x6r", [CT, D], F32, "ExternalInput")
    sS_in = _fio(fz, nc, "sS", [NU, D, UT], F32, "ExternalInput")
    sP_in = _fio(fz, nc, "sP", [NU, D, UT], F32, "ExternalInput")
    cT_in = _fio(fz, nc, "cT", [NU, D, UT], F32, "ExternalInput")
    bs_in = _fio(fz, nc, "bndS", [16, 128, 8], F32, "ExternalInput")
    bp_in = _fio(fz, nc, "bndPp", [16, 128, 8], F32, "ExternalInput")
    wo_in = _fio(fz, nc, "Wout", [D, D], F32, "ExternalInput")
    fs_in = _fio(fz, nc, "fsig", [1, 2], F32, "ExternalInput")
    identf_in = _fio(fz, nc, "ident", [128, 128], F32, "ExternalInput")
    x7_o = _fio(fz, nc, "x7r", [CT, D], F32, "ExternalOutput")

    with stack:
        sb = lambda nm, shp, dt=BF16: stack.enter_context(nc.sbuf_tensor(_nm(fz, nm), shp, dt))
        psw = lambda nm, shp, dt=F32: stack.enter_context(nc.psum_tensor(_nm(fz, nm), shp, dt))
        identb = sb("identb", [128, 128])
        wraw = sb("wraw", [128, D])
        woT = sb("woT", [128, 8 * D])
        bs = sb("bs", [128, 16 * 8], F32)
        bpp = sb("bpp", [128, 16 * 8], F32)
        sin_u = sb("sin_u", [128, NU * 8], F32)
        tmp8 = sb("tmp8", [128, 8], F32)
        esb = sb("esb", [128, UT], F32)
        epb = sb("epb", [128, UT], F32)
        ecb = sb("ecb", [128, UT], F32)
        zb = sb("zb", [128, UT], F32)
        yh = sb("yh", [128, 8 * UT])
        gs1 = sb("gs1", [1, 2], F32)
        fo128 = sb("fo128", [128, 1], F32)
        ones_row = sb("ones_row", [1, 128], F32)
        xblk = [sb(f"xblk{j}", [128, D], F32) for j in range(2)]
        xout = [sb(f"xout{j}", [128, D], F32) for j in range(2)]
        ptp = [psw(f"ptp{j}", [128, 1024], BF16) for j in range(2)]
        pmm = [psw(f"pmm{j}", [128, 512]) for j in range(2)]
        ptf = psw("ptf", [128, 128])
        p = _fz_prog(fz, nc, stack)

        p.dma("gpsimd", identb[:], identf_in[:], writes=["identb"])
        p.step("vector", lambda e: e.memset(ones_row[:], 1.0), writes=["ones"])
        p.dma("sync", gs1[0:1, 0:2], fs_in[:], writes=["gs"])
        p.step("tensor", lambda e: e.matmul(out=ptf[:, 0:1], lhsT=ones_row[:], rhs=gs1[0:1, 1:2],
                                            start=True, stop=True),
               reads=["gs", "ones"], writes=["ptf"])
        p.step("vector", lambda e: e.tensor_copy(out=fo128[:], in_=ptf[:, 0:1]),
               reads=["ptf"], writes=["fo128"])
        p.dma("sync", bs[:].rearrange("p (g j) -> p g j", g=16), bs_in.rearrange("g p j -> p g j"),
              writes=["bs"])
        p.dma("sync", bpp[:].rearrange("p (g j) -> p g j", g=16), bp_in.rearrange("g p j -> p g j"),
              writes=["bpp"])
        for k in range(8):
            p.dma("gpsimd", wraw[:], wo_in[ts(k, 128), :], writes=["wraw"])
            for kk in range(8):
                pt = ptp[kk % 2]
                ptk = ("ptp", kk % 2)
                p.step("tensor", lambda e, pt=pt, kk=kk: e.transpose(
                    out=pt[:, 0:128], in_=wraw[:, ts(kk, 128)], identity=identb[:]),
                    reads=["wraw", "identb"], writes=[ptk])
                p.step("vector", lambda e, pt=pt, k=k, kk=kk: e.tensor_copy(
                    out=woT[:, kk * 1024 + k * 128: kk * 1024 + (k + 1) * 128], in_=pt[:, 0:128]),
                    reads=[ptk], writes=[("woT", kk, k)])
        for u in range(NU):
            p.step("vector", lambda e, u=u: e.memset(sin_u[:, ts(u, 8)], 0.0), writes=[("sin", u)])
            for j in range(8):
                p.step("vector", lambda e, u=u, j=j: e.tensor_tensor(
                    out=tmp8[:], in0=sin_u[:, ts(u, 8)],
                    in1=bpp[:].rearrange("p (u j c) -> p u j c", u=NU, j=8)[:, u, j, :], op=ALU.mult),
                    reads=[("sin", u), "bpp"], writes=["tmp8"])
                p.step("vector", lambda e, u=u, j=j: e.tensor_tensor(
                    out=sin_u[:, ts(u, 8)], in0=tmp8[:],
                    in1=bs[:].rearrange("p (u j c) -> p u j c", u=NU, j=8)[:, u, j, :], op=ALU.add),
                    reads=["tmp8", "bs"], writes=[("sin", u)])
        for u in range(NU):
            for dc in range(8):
                p.dma("sync", esb[:], sS_in[u, ts(dc, 128), :], writes=["esb"])
                p.dma("sync", epb[:], sP_in[u, ts(dc, 128), :], writes=["epb"])
                p.dma("sync", ecb[:], cT_in[u, ts(dc, 128), :], writes=["ecb"])
                p.step("vector", lambda e, u=u, dc=dc: e.scalar_tensor_tensor(
                    out=esb[:], in0=epb[:], scalar=sin_u[:, u * 8 + dc: u * 8 + dc + 1],
                    in1=esb[:], op0=ALU.mult, op1=ALU.add),
                    reads=["esb", "epb", ("sin", u)], writes=["esb"])
                p.step("vector", lambda e: e.tensor_copy(out=zb[:, 1:1024], in_=esb[:, 0:1023]),
                       reads=["esb"], writes=["zb"])
                p.step("vector", lambda e, u=u, dc=dc: e.tensor_copy(
                    out=zb[:, 0:1], in_=sin_u[:, u * 8 + dc: u * 8 + dc + 1]),
                    reads=[("sin", u)], writes=["zb"])
                p.step("vector", lambda e, dc=dc: e.tensor_tensor(
                    out=yh[:, ts(dc, UT)], in0=ecb[:], in1=zb[:], op=ALU.mult),
                    reads=["ecb", "zb"], writes=[("yh", dc)])
            for tb in range(8):
                gi = u * 8 + tb
                xb, xo = xblk[tb % 2], xout[tb % 2]
                xbk, xok = ("xblk", tb % 2), ("xout", tb % 2)
                p.dma("sync", xb[:], x6[ts(gi, 128), :], writes=[xbk])
                for half in range(2):
                    pm = pmm[half % 2]
                    pmk = ("pmm", half % 2)
                    for dc in range(8):
                        p.step("tensor", lambda e, pm=pm, dc=dc, tb=tb, half=half: e.matmul(
                            out=pm[:], lhsT=yh[:, dc * UT + tb * 128: dc * UT + (tb + 1) * 128],
                            rhs=woT[:, dc * 1024 + half * 512: dc * 1024 + (half + 1) * 512],
                            start=(dc == 0), stop=(dc == 7)),
                            reads=[("yh", dc)] + [("woT", dc, k) for k in range(8)], writes=[pmk])
                    p.step("vector", lambda e, pm=pm, xo=xo, xb=xb, half=half: e.scalar_tensor_tensor(
                        out=xo[:, ts(half, 512)], in0=pm[:], scalar=fo128[:, 0:1],
                        in1=xb[:, ts(half, 512)], op0=ALU.mult, op1=ALU.add),
                        reads=[pmk, xbk, "fo128"], writes=[(xok, half)])
                p.store(x7_o[ts(gi, 128), :], xo[:], reads=[(xok, 0), (xok, 1)])
        _fz_end(fz, p)
    return nc



def build_v9(fz=None):
    nc, stack, fz = _fz_begin(fz)
    xa = _fio(fz, nc, "xa", [CT, D], F32, "ExternalInput")
    xb = _fio(fz, nc, "xb", [CT, D], F32, "ExternalInput")
    lno = _fio(fz, nc, "lno", [CT, D], F32, "ExternalOutput")
    smo = _fio(fz, nc, "smo", [CT, D], F32, "ExternalOutput")
    with stack:
        sbf = lambda nm, shp, dt=F32: stack.enter_context(nc.sbuf_tensor(_nm(fz, nm), shp, dt))
        xab = [sbf(f"v9xa{j}", [128, D]) for j in range(2)]
        xbb = [sbf(f"v9xb{j}", [128, D]) for j in range(2)]
        lnb = [sbf(f"v9ln{j}", [128, D]) for j in range(2)]
        smb = [sbf(f"v9sm{j}", [128, D]) for j in range(2)]
        p = _fz_prog(fz, nc, stack)
        cpool = ConstPool(nc, stack, p, tag=_nm(fz, 'v9cp'))
        lnsc = ln_scratch(nc, stack, tag=_nm(fz, 'V9'), cpool=cpool)
        for blk in range(NB):
            j = blk % 2
            xak, xbk, lnk, smk = ("v9xa", j), ("v9xb", j), ("v9ln", j), ("v9sm", j)
            p.dma("sync", xab[j][:], xa[ts(blk, 128), :], writes=[xak])
            p.dma("sync", xbb[j][:], xb[ts(blk, 128), :], writes=[xbk])
            emit_ln(p, xab[j][:], [xak], lnb[j][:], lnk, lnsc)
            p.step("vector", lambda e, j=j: e.tensor_tensor(
                out=smb[j][:], in0=xab[j][:], in1=xbb[j][:], op=ALU.add),
                reads=[xak, xbk], writes=[smk])
            p.store(lno[ts(blk, 128), :], lnb[j][:], reads=[lnk])
            p.store(smo[ts(blk, 128), :], smb[j][:], reads=[smk])
        _fz_end(fz, p)
    return nc


def v9maps(Xa, Xb):
    return [{"xa": _rows(Xa, i), "xb": _rows(Xb, i)} for i in range(NCORE)]



def _to128(vec, w):
    return np.ascontiguousarray(np.asarray(vec, np.float32).reshape(w, 128).T)


def _rows(X, i):
    us = units_for_core(i)
    return np.ascontiguousarray(np.concatenate(
        [X[b, 1024 * c:1024 * (c + 1)] for (b, c) in us], axis=0))


def _scatter_rows(res, name):
    X = np.zeros((B, T, D), np.float32)
    for i in range(NCORE):
        r = res[i][name]
        for ui, (b, c) in enumerate(units_for_core(i)):
            X[b, 1024 * c:1024 * (c + 1)] = r[ts(ui, UT)]
    return X


def kernel(**inputs):
    inp = {k: np.asarray(v) for k, v in inputs.items()}
    X = np.ascontiguousarray(inp["L0_attn_a_x"].astype(np.float32, copy=False))

    nc = build_v7()
    res = _run(nc, [{"Win": inp["L1_attn_b_W_in"], "Wout": inp["L1_attn_b_W_out"],
                     "ident": _IDENT}] * NCORE)
    fsig = np.ascontiguousarray(res.results[0]["fsig"])

    qT, kT, va = run_v1(X, inp)
    nc = build_v2()
    maps = host_attn_streams(qT, kT, va)
    for i, mp in enumerate(maps):
        mp["xr"] = _rows(X, i)
        mp["wo"] = inp["L0_attn_a_w_o"]
        mp["ident"] = _IDENT
    res = _run(nc, maps)
    X = _scatter_rows(res.results, "x1r")

    nc = build_v3()
    maps = [{"x1r": _rows(X, i), "fw": inp["L0_ffn_weight"],
             "falpha": inp["L0_ffn_alpha"].reshape(1, 1),
             "fbias": inp["L0_ffn_bias"].reshape(1, D),
             "W1": inp["L0_attn_b_W1"], "b1": inp["L0_attn_b_b1"].reshape(6144, 1),
             "ident": _IDENT} for i in range(NCORE)]
    res = _run(nc, maps)
    X = _scatter_rows(res.results, "x2r")
    scanH = np.zeros((B, 8, 2048, UT), np.float32)
    scanP = np.zeros((B, 8, 2048, UT), np.float32)
    for i in range(NCORE):
        for ui, (b, c) in enumerate(units_for_core(i)):
            scanH[b, c] = res.results[i]["scanH"][ui]
            scanP[b, c] = res.results[i]["scanP"][ui]

    nc = build_v4()
    maps = []
    for i in range(NCORE):
        us = units_for_core(i)
        pbH = np.zeros((NU, 8, 128, 16), np.float32)
        pbP = np.ones((NU, 8, 128, 16), np.float32)
        for ui, (b, c) in enumerate(us):
            for j in range(c):
                pbH[ui, j] = _to128(scanH[b, j][:, -1], 16)
                pbP[ui, j] = _to128(scanP[b, j][:, -1], 16)
        maps.append({"x2r": _rows(X, i),
                     "scanH": np.stack([scanH[b, c] for (b, c) in us]),
                     "scanP": np.stack([scanP[b, c] for (b, c) in us]),
                     "bndH": pbH.reshape(16, 128, 16), "bndP": pbP.reshape(16, 128, 16),
                     "inith": _to128(inp["L0_attn_b_init_h"], 16),
                     "W2": inp["L0_attn_b_W2"], "b2": inp["L0_attn_b_b2"].reshape(1, D),
                     "fw": inp["L0_ffn_weight"],
                     "falpha": inp["L0_ffn_alpha"].reshape(1, 1),
                     "fbias": inp["L0_ffn_bias"].reshape(1, D),
                     "declog": _to128(inp["L1_attn_a_decay_logit"], 8),
                     "ident": _IDENT})
    res = _run(nc, maps)
    X = _scatter_rows(res.results, "x4r")
    emaS = np.zeros((B, 8, D, UT), np.float32)
    emaP = np.zeros((B, 8, D, UT), np.float32)
    for i in range(NCORE):
        for ui, (b, c) in enumerate(units_for_core(i)):
            emaS[b, c] = res.results[i]["emaS"][ui]
            emaP[b, c] = res.results[i]["emaP"][ui]

    nc = build_v5()
    maps = []
    for i in range(NCORE):
        us = units_for_core(i)
        pbS = np.zeros((NU, 8, 128, 8), np.float32)
        pbP = np.ones((NU, 8, 128, 8), np.float32)
        halo = np.zeros((NU, 4, D), np.float32)
        for ui, (b, c) in enumerate(us):
            for j in range(c):
                pbS[ui, j] = _to128(emaS[b, j][:, -1], 8)
                pbP[ui, j] = _to128(emaP[b, j][:, -1], 8)
            if c > 0:
                halo[ui, 0:2] = X[b, 1024 * c - 2:1024 * c]
            if c < 7:
                halo[ui, 2:4] = X[b, 1024 * (c + 1):1024 * (c + 1) + 2]
        maps.append({"x4r": _rows(X, i),
                     "emaS": np.stack([emaS[b, c] for (b, c) in us]),
                     "emaP": np.stack([emaP[b, c] for (b, c) in us]),
                     "bndS": pbS.reshape(16, 128, 8), "bndPp": pbP.reshape(16, 128, 8),
                     "W": inp["L1_attn_a_W"], "b": inp["L1_attn_a_b"].reshape(1, D),
                     "convw": inp["L1_attn_a_conv_w"].reshape(D, 5),
                     "convb": inp["L1_attn_a_conv_b"].reshape(D, 1),
                     "halo": halo, "ident": _IDENT})
    res = _run(nc, maps)
    X = _scatter_rows(res.results, "x5r")

    nc = build_v9()
    res = _run(nc, v9maps(X, X))
    Xi = _scatter_rows(res.results, "lno")
    for l in range(2):
        nc = build_v6()
        res = _run(nc, host_v6_maps(Xi, l, inp))
        Xi = _scatter_rows(res.results, "xor")
    nc = build_v9()
    res = _run(nc, v9maps(X, Xi))
    X = _scatter_rows(res.results, "smo")

    nc = build_v8a()
    maps = [{"x6r": _rows(X, i), "Wp": inp["L1_attn_b_Wp"],
             "bp": inp["L1_attn_b_bp"].reshape(3072, 1),
             "Win": inp["L1_attn_b_W_in"],
             "alog": _to128(inp["L1_attn_b_alpha_log"], 8),
             "dbias": _to128(inp["L1_attn_b_delta_bias"], 8),
             "lgam": inp["L1_attn_b_log_gamma"].reshape(1, 1),
             "fsig": fsig, "ident": _IDENT} for i in range(NCORE)]
    res = _run(nc, maps)
    sS = np.zeros((B, 8, D, UT), np.float32)
    sP = np.zeros((B, 8, D, UT), np.float32)
    cT = np.zeros((B, 8, D, UT), np.float32)
    for i in range(NCORE):
        for ui, (b, c) in enumerate(units_for_core(i)):
            sS[b, c] = res.results[i]["sS"][ui]
            sP[b, c] = res.results[i]["sP"][ui]
            cT[b, c] = res.results[i]["cT"][ui]
    nc = build_v8b()
    maps = []
    for i in range(NCORE):
        us = units_for_core(i)
        pbS = np.zeros((NU, 8, 128, 8), np.float32)
        pbP = np.ones((NU, 8, 128, 8), np.float32)
        for ui, (b, c) in enumerate(us):
            for j in range(c):
                pbS[ui, j] = _to128(sS[b, j][:, -1], 8)
                pbP[ui, j] = _to128(sP[b, j][:, -1], 8)
        maps.append({"x6r": _rows(X, i),
                     "sS": np.stack([sS[b, c] for (b, c) in us]),
                     "sP": np.stack([sP[b, c] for (b, c) in us]),
                     "cT": np.stack([cT[b, c] for (b, c) in us]),
                     "bndS": pbS.reshape(16, 128, 8), "bndPp": pbP.reshape(16, 128, 8),
                     "Wout": inp["L1_attn_b_W_out"], "fsig": fsig, "ident": _IDENT})
    res = _run(nc, maps)
    X = _scatter_rows(res.results, "x7r")

    nc = build_v9()
    res = _run(nc, v9maps(X, X))
    Xi = _scatter_rows(res.results, "lno")
    for l in range(2):
        nc = build_v6()
        res = _run(nc, host_v6_maps(Xi, l, inp))
        Xi = _scatter_rows(res.results, "xor")
    nc = build_v9()
    res = _run(nc, v9maps(X, Xi))
    X = _scatter_rows(res.results, "smo")

    return X



from concourse.bass import IndirectOffsetOnAxis


def build_fused():
    fzc = Fz()
    nc = fzc.nc
    gs = fzc.gstack
    ZR = NCORE * CT
    with gs:
        fzc.p = Prog(nc, gs)
        p = fzc.p
        dt_ = lambda nm, shp, dt, kind: nc.dram_tensor(nm, shp, dt, kind=kind).ap()
        E = {}
        def ein(nm, shp, dt=F32):
            E[nm] = dt_(nm, shp, dt, "ExternalInput")
            return E[nm]
        xr0 = ein("xr", [CT, D])
        ident = ein("ident", [128, 128])
        for nm, shp in (("wq", [H, DK, D]), ("wk", [H, DK, D]), ("wv", [H, DV, D]),
                        ("wo0", [D, D]), ("bias2", [1, NU * 8]),
                        ("fw", [D, D]), ("falpha", [1, 1]), ("fbias", [1, D]),
                        ("W1m", [D, 6144]), ("b1m", [6144, 1]),
                        ("inith", [128, 16]), ("W2m", [2048, D]), ("b2m", [1, D]),
                        ("declog", [128, 8]),
                        ("Wa", [D, D]), ("ba", [1, D]), ("convw", [D, 5]), ("convb", [D, 1]),
                        ("Wp", [3072, D]), ("bpv", [3072, 1]), ("Winm", [D, D]),
                        ("alogv", [128, 8]), ("dbiasv", [128, 8]), ("lgam", [1, 1]),
                        ("Woutm", [D, D]), ("Wsig", [D, D]),
                        ("wmaskv", [1, NU * 4]),
                        ("csqv", [NU, 2048, 512]), ("ssqv", [NU, 2048, 512]),
                        ("cskv", [NU, 2048, 512]), ("sskv", [NU, 2048, 512]),
                        ("ohpad", [128, NU * 8 * 16]), ("padflag", [128, NU * 8])):
            ein(nm, shp)
        for l in range(2):
            for nm, shp in (("wqkv", [D, 1536]), ("wol", [512, D]), ("lng", [1, D]),
                            ("lnb", [1, D]), ("w1l", [D, 5460]), ("b1l", [5460, 1]),
                            ("w2l", [2730, D]), ("b2l", [1, D]),
                            ("qsrow", [1, 512]), ("ksrow", [1, 512])):
                ein(f"{nm}{l}", shp)
        for nm, shp in (("idxKV", [128, NU * 8 * 2]),
                        ("idxXA", [128, NU * 4]),
                        ("idxXB", [4, 1]),
                        ("idxH", [8, 1])):
            E[nm] = dt_(nm, shp, mybir.dt.uint32, "ExternalInput")
        out_x = dt_("out", [CT, D], F32, "ExternalOutput")
        I = {}
        def itn(nm, shp, dt=F32):
            I[nm] = dt_(nm, shp, dt, "Internal")
            return I[nm]
        for k in range(1, 8):
            itn(f"xs{k}", [CT, D])
        itn("xs6a", [CT, D]); itn("xs8a", [CT, D])
        itn("qT_l", [NU, H, DK, UT], BF16)
        itn("kT_l", [NU, H, DK, UT], BF16)
        itn("va_l", [NU, UT, H * 65], BF16)
        itn("kAG", [NCORE * NU * H * DK, UT], BF16)
        itn("vAG", [NCORE * NU * UT, H * 65], BF16)
        itn("kstrv", [NU, H, DK, 8 * 1024], BF16)
        itn("vstrv", [NU, 8 * 1024, H * 65], BF16)
        itn("scanH", [NU, 2048, UT]); itn("scanP", [NU, 2048, UT])
        itn("bndlH", [128, NU * 16]); itn("bndlP", [128, NU * 16])
        itn("bndHAG", [128 * NCORE, NU * 16]); itn("bndPAG", [128 * NCORE, NU * 16])
        itn("pbndH", [16, 128, 16]); itn("pbndP", [16, 128, 16])
        itn("emaS", [NU, D, UT]); itn("emaP", [NU, D, UT])
        itn("bndlS", [128, NU * 8]); itn("bndlPe", [128, NU * 8])
        itn("bndSAG", [128 * NCORE, NU * 8]); itn("bndPeAG", [128 * NCORE, NU * 8])
        itn("pbndS", [16, 128, 8]); itn("pbndPe", [16, 128, 8])
        itn("bndX", [NU * 4 + 2, D]); itn("bndXAG", [NCORE * (NU * 4 + 2), D])
        itn("halo", [NU, 4, D])
        itn("xg", [NCORE * CT + 1024, D])
        itn("xallv", [NU, 2050, D])
        itn("xmid_scr", [NU, 12 * 128, D])
        itn("sSs", [NU, D, UT]); itn("sPs", [NU, D, UT]); itn("cTs", [NU, D, UT])
        itn("bndlS2", [128, NU * 8]); itn("bndlP2", [128, NU * 8])
        itn("bndS2AG", [128 * NCORE, NU * 8]); itn("bndP2AG", [128 * NCORE, NU * 8])
        itn("pbndS2", [16, 128, 8]); itn("pbndP2", [16, 128, 8])
        itn("fsigloc", [1, 1]); itn("sigAG", [NCORE, 1])

        _uc = [0]
        def _u():
            _uc[0] += 1
            return f"glu{_uc[0]}"

        def ag(in_ap, out_ap, rkeys, wkey):
            p.step("gpsimd", lambda e: e.collective_compute(
                "AllGather", ALU.bypass, replica_groups=[list(range(NCORE))],
                ins=[in_ap], outs=[out_ap]),
                sem="CC", amt=16, reads=rkeys, writes=[wkey])

        def stage(builder, m):
            fzc.maps = dict(m)
            fzc.maps.setdefault("ident", ident)
            builder(fzc)

        stage(build_v7, {"Wsig": E["Wsig"], "fsigloc": I["fsigloc"]})
        ag(I["fsigloc"][:], I["sigAG"][:], ["fsigloc_d"], "sigAG_d")
        fsig_ap = I["sigAG"].rearrange("a b -> b a")[:, 0:2]

        stage(build_v1, {"xr": xr0, "wq": E["wq"], "wk": E["wk"], "wv": E["wv"],
                         "qT": I["qT_l"].rearrange("u h d t -> h d u t"),
                         "kT": I["kT_l"].rearrange("u h d t -> h d u t"),
                         "vaug": I["va_l"].rearrange("u t c -> (u t) c")})
        ag(I["kT_l"].rearrange("u h d t -> (u h d) t"), I["kAG"][:], [], "kAG_d")
        ag(I["va_l"].rearrange("u t c -> (u t) c"), I["vAG"][:], [], "vAG_d")

        with ExitStack() as st2:
            kt = st2.enter_context(nc.sbuf_tensor("gl_kt", [128, 8 * 1024], BF16))
            vt = st2.enter_context(nc.sbuf_tensor("gl_vt", [128, 8 * 1040], BF16))
            idxkv = st2.enter_context(nc.sbuf_tensor("gl_idxkv", [128, NU * 16], mybir.dt.uint32))
            p.dma("sync", idxkv[:], E["idxKV"][:], writes=["idxkv"])
            for u in range(NU):
                for j in range(8):
                    qk = (u * 8 + j) * 2
                    p.step("gpsimd", lambda e, qk=qk: e.indirect_dma_start(
                        out=kt[:], out_offset=None, in_=I["kAG"][:],
                        in_offset=IndirectOffsetOnAxis(idxkv[:, qk:qk + 1], 0)),
                        sem="LDG", amt=16, reads=["idxkv", "kAG_d"], writes=["gl_kt"])
                    p.dma("sync",
                          I["kstrv"][u].rearrange("h d t -> (h d) t")[:, j * 1024:(j + 1) * 1024]
                          .rearrange("(pp q) t -> pp q t", q=8),
                          kt[:].rearrange("pp (q t) -> pp q t", q=8),
                          reads=["gl_kt"], writes=[("kstrv", u, j)])
                    p.step("gpsimd", lambda e, qk=qk: e.indirect_dma_start(
                        out=vt[:], out_offset=None, in_=I["vAG"][:],
                        in_offset=IndirectOffsetOnAxis(idxkv[:, qk + 1:qk + 2], 0)),
                        sem="LDG", amt=16, reads=["idxkv", "vAG_d"], writes=["gl_vt"])
                    p.dma("sync",
                          I["vstrv"][u, j * 1024:(j + 1) * 1024, :]
                          .rearrange("(pp q) c -> pp q c", q=8),
                          vt[:].rearrange("pp (q c) -> pp q c", q=8),
                          reads=["gl_vt"], writes=[("vstrv", u, j)])
            p.stage_barrier()

        stage(build_v2, {
            "qTu": I["qT_l"][:], "kstr": I["kstrv"], "vstr": I["vstrv"],
            "bias": E["bias2"], "xr": xr0, "wo": E["wo0"], "x1r": I["xs1"]})

        stage(build_v3, {"x1r": I["xs1"], "fw": E["fw"], "falpha": E["falpha"],
                         "fbias": E["fbias"], "W1": E["W1m"], "b1": E["b1m"],
                         "x2r": I["xs2"], "scanH": I["scanH"], "scanP": I["scanP"]})

        def bnd_extract_build(src_names, ncols, bndl_names, bag_names, pb_names):
            W = NU * ncols
            with ExitStack() as st3:
                stg = st3.enter_context(nc.sbuf_tensor(_u(), [128, 2 * W], F32))
                bgt = st3.enter_context(nc.sbuf_tensor(_u(), [128, 2 * 16 * ncols], F32))
                pbt = st3.enter_context(nc.sbuf_tensor(_u(), [128, ncols], F32))
                ohp = st3.enter_context(nc.sbuf_tensor(_u(), [128, NU * 8 * 16], F32))
                pfl = st3.enter_context(nc.sbuf_tensor(_u(), [128, NU * 8], F32))
                p.dma("sync", ohp[:], E["ohpad"][:], writes=["ohp"])
                p.dma("sync", pfl[:], E["padflag"][:], writes=["pfl"])
                for si, src in enumerate(src_names):
                    for u in range(NU):
                        for cc in range(ncols):
                            p.dma("sync", stg[:, si * W + u * ncols + cc: si * W + u * ncols + cc + 1],
                                  I[src][u, ts(cc, 128), UT - 1:UT],
                                  writes=[("bstg", si, u, cc)],
                                  allow_slow_non_contiguous=True)
                    p.dma("sync", I[bndl_names[si]][:], stg[:, si * W:(si + 1) * W],
                          reads=[("bstg", si, u, cc) for u in range(NU) for cc in range(ncols)],
                          writes=[("bndl", si)])
                    ag(I[bndl_names[si]][:], I[bag_names[si]][:], [("bndl", si)], ("bag", si))
                    for g in range(16):
                        p.dma("sync", bgt[:, (si * 16 + g) * ncols:(si * 16 + g + 1) * ncols],
                              I[bag_names[si]][128 * (g // 2):128 * (g // 2) + 128,
                                               ncols * (g % 2):ncols * (g % 2) + ncols],
                              reads=[("bag", si)], writes=[("bgt", si, g)])
                for si in range(2):
                    for u in range(NU):
                        for j in range(8):
                            p.step("vector", lambda e: e.memset(pbt[:], 0.0), writes=["pbt"])
                            for g in range(16):
                                col = (u * 8 + j) * 16 + g
                                p.step("vector", lambda e, si=si, g=g, col=col: e.scalar_tensor_tensor(
                                    out=pbt[:], in0=bgt[:, (si * 16 + g) * ncols:(si * 16 + g + 1) * ncols],
                                    scalar=ohp[:, col:col + 1], in1=pbt[:],
                                    op0=ALU.mult, op1=ALU.add),
                                    reads=[("bgt", si, g), "ohp", "pbt"], writes=["pbt"])
                            if si == 1:
                                p.step("vector", lambda e, u=u, j=j: e.tensor_single_scalar(
                                    out=pbt[:], in_=pbt[:], scalar=pfl[:, u * 8 + j:u * 8 + j + 1],
                                    op=ALU.add),
                                    reads=["pbt", "pfl"], writes=["pbt"])
                            p.dma("sync", I[pb_names[si]][u * 8 + j], pbt[:],
                                  reads=["pbt"], writes=[("pb", si, u, j)])
                p.stage_barrier()

        bnd_extract_build(["scanH", "scanP"], 16, ["bndlH", "bndlP"],
                          ["bndHAG", "bndPAG"], ["pbndH", "pbndP"])

        stage(build_v4, {"x2r": I["xs2"], "scanH": I["scanH"], "scanP": I["scanP"],
                         "bndH": I["pbndH"], "bndP": I["pbndP"], "inith": E["inith"],
                         "W2": E["W2m"], "b2": E["b2m"], "fw": E["fw"],
                         "falpha": E["falpha"], "fbias": E["fbias"],
                         "declog": E["declog"], "x4r": I["xs4"],
                         "emaS": I["emaS"], "emaP": I["emaP"]})
        bnd_extract_build(["emaS", "emaP"], 8, ["bndlS", "bndlPe"],
                          ["bndSAG", "bndPeAG"], ["pbndS", "pbndPe"])

        with ExitStack() as st4:
            xt2 = st4.enter_context(nc.sbuf_tensor(_u(), [8, D], F32))
            idxh = st4.enter_context(nc.sbuf_tensor(_u(), [8, 1], mybir.dt.uint32))
            p.dma("sync", idxh[0:8, :], E["idxH"][:], writes=["idxh"])
            p.step("vector", lambda e: e.memset(xt2[0:8, :], 0.0), writes=["xt2z"])
            p.dma("sync", I["bndX"][NU * 4:NU * 4 + 2, :], xt2[2:4, :],
                  reads=["xt2z"], writes=["bndXz"])
            for u in range(NU):
                for side, r0 in ((0, 0), (1, UT - 2)):
                    p.dma("sync", xt2[0:2, :], I["xs4"][u * UT + r0: u * UT + r0 + 2, :],
                          reads=[], writes=["xt2z"])
                    p.dma("sync", I["bndX"][u * 4 + side * 2: u * 4 + side * 2 + 2, :],
                          xt2[0:2, :],
                          reads=["xt2z"], writes=[("bndX", u, side)])
            ag(I["bndX"][:], I["bndXAG"][:],
               [("bndX", u, s_) for u in range(NU) for s_ in range(2)] + ["bndXz"],
               "bndXAG_d")
            p.step("gpsimd", lambda e: e.indirect_dma_start(
                out=xt2[0:8, :], out_offset=None, in_=I["bndXAG"][:],
                in_offset=IndirectOffsetOnAxis(idxh[0:8, 0:1], 0)),
                sem="LDG", amt=16, reads=["idxh", "bndXAG_d", "xt2z"], writes=["xt2g"])
            p.dma("sync", I["halo"].rearrange("u r c -> (u r) c"), xt2[0:8, :],
                  reads=["xt2g"], writes=["halo_d"])
            p.stage_barrier()

        stage(build_v5, {"x4r": I["xs4"], "emaS": I["emaS"], "emaP": I["emaP"],
                         "bndS": I["pbndS"], "bndPp": I["pbndPe"],
                         "W": E["Wa"], "b": E["ba"], "convw": E["convw"],
                         "convb": E["convb"], "halo": I["halo"], "x5r": I["xs5"]})

        with ExitStack() as st6:
            zt = st6.enter_context(nc.sbuf_tensor(_u(), [128, D], F32))
            p.step("vector", lambda e: e.memset(zt[:], 0.0), writes=["zt"])
            for r in range(8):
                p.dma("sync", I["xg"][ZR + r * 128: ZR + (r + 1) * 128, :], zt[:],
                      reads=["zt"], writes=[("xgz", r)])
            p.stage_barrier()

        def xall_rebuild(src, tag):
            with ExitStack() as st5:
                xt = st5.enter_context(nc.sbuf_tensor(_u(), [128, 16 * D], F32))
                xtb = st5.enter_context(nc.sbuf_tensor(_u(), [4, D], F32))
                idxx = st5.enter_context(nc.sbuf_tensor(_u(), [128, NU * 4], mybir.dt.uint32))
                idxb = st5.enter_context(nc.sbuf_tensor(_u(), [4, 1], mybir.dt.uint32))
                p.dma("sync", idxx[:], E["idxXA"][:], writes=["idxx"])
                p.dma("sync", idxb[0:4, :], E["idxXB"][:], writes=["idxb"])
                ag(I[src][:], I["xg"][0:NCORE * CT, :], [], ("xg_d", tag))
                for u in range(NU):
                    p.step("gpsimd", lambda e, u=u: e.indirect_dma_start(
                        out=xt[:], out_offset=None, in_=I["xg"][:],
                        in_offset=IndirectOffsetOnAxis(idxx[:, u * 4:u * 4 + 1], 0)),
                        sem="LDG", amt=16,
                        reads=["idxx", ("xg_d", tag)] + [("xgz", r) for r in range(8)],
                        writes=["glxt"])
                    p.dma("sync", I["xallv"][u, 2:2050, :].rearrange("(pp q) c -> pp q c", q=16),
                          xt[:].rearrange("pp (q c) -> pp q c", q=16),
                          reads=["glxt"], writes=[("xallv", u)])
                p.step("gpsimd", lambda e: e.indirect_dma_start(
                    out=xtb[0:4, :], out_offset=None, in_=I["xg"][:],
                    in_offset=IndirectOffsetOnAxis(idxb[0:4, 0:1], 0)),
                    sem="LDG", amt=16,
                    reads=["idxb", ("xg_d", tag)] + [("xgz", r) for r in range(8)],
                    writes=["glxtb"])
                for u in range(NU):
                    p.dma("sync", I["xallv"][u, 0:2, :], xtb[2 * u:2 * u + 2, :],
                          reads=["glxtb"], writes=[("xallvb", u)])
                p.stage_barrier()

        def v6maps(l, xout_ap):
            return {"xall": I["xallv"], "wqkv": E[f"wqkv{l}"], "wo": E[f"wol{l}"],
                    "lng": E[f"lng{l}"], "lnb": E[f"lnb{l}"], "w1": E[f"w1l{l}"],
                    "b1": E[f"b1l{l}"], "w2": E[f"w2l{l}"], "b2": E[f"b2l{l}"],
                    "qsrow": E[f"qsrow{l}"], "ksrow": E[f"ksrow{l}"],
                    "csq": E["csqv"], "ssq": E["ssqv"], "csk": E["cskv"], "ssk": E["sskv"],
                    "wmask": E["wmaskv"], "xor": xout_ap, "xmidh": I["xmid_scr"]}

        xall_rebuild("xs5", 0)
        stage(build_v6, v6maps(0, I["xs6a"]))
        xall_rebuild("xs6a", 1)
        stage(build_v6, v6maps(1, I["xs6"]))

        stage(build_v8a, {"x6r": I["xs6"], "Wp": E["Wp"], "bp": E["bpv"],
                          "Win": E["Winm"], "alog": E["alogv"], "dbias": E["dbiasv"],
                          "lgam": E["lgam"], "fsig": fsig_ap,
                          "sS": I["sSs"], "sP": I["sPs"], "cT": I["cTs"]})
        bnd_extract_build(["sSs", "sPs"], 8, ["bndlS2", "bndlP2"],
                          ["bndS2AG", "bndP2AG"], ["pbndS2", "pbndP2"])
        stage(build_v8b, {"x6r": I["xs6"], "sS": I["sSs"], "sP": I["sPs"],
                          "cT": I["cTs"], "bndS": I["pbndS2"], "bndPp": I["pbndP2"],
                          "Wout": E["Woutm"], "fsig": fsig_ap, "x7r": I["xs7"]})

        xall_rebuild("xs7", 2)
        stage(build_v6, v6maps(0, I["xs8a"]))
        xall_rebuild("xs8a", 3)
        stage(build_v6, v6maps(1, out_x))
        p.done()
        p.finalize()
    return nc


def host_fused_maps(inp):
    X0 = np.ascontiguousarray(np.asarray(inp["L0_attn_a_x"], np.float32))
    ZR = NCORE * CT
    DH, WINDOW = 64, 512
    n = T
    half_idx = np.arange(0, DH, 2, dtype=np.float64)
    inv_freq = 1.0 / 10000.0 ** (half_idx / DH)
    tpos = np.arange(n, dtype=np.float64)
    freqs = np.concatenate([tpos[:, None] * inv_freq[None, :]] * 2, axis=-1)
    base = (half_idx + 0.4 * DH) / (1.4 * DH)
    power = (tpos - n // 2) / float(WINDOW // 2)
    scale = np.concatenate([base[None, :] ** power[:, None]] * 2, axis=-1)
    cos, sin = np.cos(freqs), np.sin(freqs)
    sgn = np.concatenate([-np.ones(DH // 2), np.ones(DH // 2)])
    csq_t = (cos * scale).astype(np.float32)
    ssq_t = (sin * scale * sgn).astype(np.float32)
    csk_t = (cos / scale).astype(np.float32)
    ssk_t = (sin / scale * sgn).astype(np.float32)
    tile8 = lambda tab: np.tile(tab, (1, 8))

    maps = []
    for i in range(NCORE):
        us = units_for_core(i)
        mp = {
            "xr": _rows(X0, i), "ident": _IDENT,
            "wq": np.asarray(inp["L0_attn_a_w_q"]), "wk": np.asarray(inp["L0_attn_a_w_k"]),
            "wv": np.asarray(inp["L0_attn_a_w_v"]), "wo0": np.asarray(inp["L0_attn_a_w_o"]),
            "fw": np.asarray(inp["L0_ffn_weight"]),
            "falpha": np.asarray(inp["L0_ffn_alpha"]).reshape(1, 1),
            "fbias": np.asarray(inp["L0_ffn_bias"]).reshape(1, D),
            "W1m": np.asarray(inp["L0_attn_b_W1"]),
            "b1m": np.asarray(inp["L0_attn_b_b1"]).reshape(6144, 1),
            "inith": _to128(inp["L0_attn_b_init_h"], 16),
            "W2m": np.asarray(inp["L0_attn_b_W2"]),
            "b2m": np.asarray(inp["L0_attn_b_b2"]).reshape(1, D),
            "declog": _to128(inp["L1_attn_a_decay_logit"], 8),
            "Wa": np.asarray(inp["L1_attn_a_W"]),
            "ba": np.asarray(inp["L1_attn_a_b"]).reshape(1, D),
            "convw": np.asarray(inp["L1_attn_a_conv_w"]).reshape(D, 5),
            "convb": np.asarray(inp["L1_attn_a_conv_b"]).reshape(D, 1),
            "Wp": np.asarray(inp["L1_attn_b_Wp"]),
            "bpv": np.asarray(inp["L1_attn_b_bp"]).reshape(3072, 1),
            "Winm": np.asarray(inp["L1_attn_b_W_in"]),
            "alogv": _to128(inp["L1_attn_b_alpha_log"], 8),
            "dbiasv": _to128(inp["L1_attn_b_delta_bias"], 8),
            "lgam": np.asarray(inp["L1_attn_b_log_gamma"]).reshape(1, 1),
            "Woutm": np.asarray(inp["L1_attn_b_W_out"]),
            "Wsig": np.asarray(inp["L1_attn_b_W_in"] if i != 1 else inp["L1_attn_b_W_out"]),
        }
        for l in range(2):
            mp[f"wqkv{l}"] = np.asarray(inp["L1_ffn_wqkv"][l])
            mp[f"wol{l}"] = np.asarray(inp["L1_ffn_wo"][l])
            mp[f"lng{l}"] = np.asarray(inp["L1_ffn_ln_g"][l])[None, :]
            mp[f"lnb{l}"] = np.asarray(inp["L1_ffn_ln_b"][l])[None, :]
            mp[f"w1l{l}"] = np.asarray(inp["L1_ffn_w1"][l])
            mp[f"b1l{l}"] = np.asarray(inp["L1_ffn_b1"][l])[:, None]
            mp[f"w2l{l}"] = np.asarray(inp["L1_ffn_w2"][l])
            mp[f"b2l{l}"] = np.asarray(inp["L1_ffn_b2"][l])[None, :]
            mp[f"qsrow{l}"] = np.tile(np.asarray(inp["L1_ffn_q_scale"][l]), 8)[None, :]
            mp[f"ksrow{l}"] = np.tile(np.asarray(inp["L1_ffn_k_scale"][l]), 8)[None, :]
        bias2 = np.zeros((1, NU * 8), np.float32)
        wm = np.zeros((1, NU * 4), np.float32)
        csq = np.zeros((NU, 2048, 512), np.float32)
        ssq = np.zeros((NU, 2048, 512), np.float32)
        csk = np.zeros((NU, 2048, 512), np.float32)
        ssk = np.zeros((NU, 2048, 512), np.float32)
        idxKV = np.zeros((128, NU * 16), np.uint32)
        idxXA = np.zeros((128, NU * 4), np.uint32)
        idxXB = np.zeros((4, 1), np.uint32)
        idxH = np.zeros((8, 1), np.uint32)
        ohpad = np.zeros((128, NU * 8 * 16), np.float32)
        padflag = np.zeros((128, NU * 8), np.float32)
        pvec = np.arange(128)
        for ui, (b, c) in enumerate(us):
            g = b * 8 + c
            s = 1024 * g
            bias2[0, ui * 8: ui * 8 + 8] = [0.0 if j <= c else -30000.0 for j in range(8)]
            for w4 in range(4):
                gw = 2 * c - 2 + w4
                wm[0, ui * 4 + w4] = -30000.0 if gw == 0 else 0.0
            pidx = np.clip(np.arange(s - 8192 * b - 1024, s - 8192 * b + 1024), 0, n - 1)
            csq[ui] = tile8(csq_t[pidx]); ssq[ui] = tile8(ssq_t[pidx])
            csk[ui] = tile8(csk_t[pidx]); ssk[ui] = tile8(ssk_t[pidx])
            for j in range(8):
                jc = max(c - j, 0)
                gsrc = b * 8 + jc
                idxKV[:, (ui * 8 + j) * 2] = gsrc * 1024 + 8 * pvec
                idxKV[:, (ui * 8 + j) * 2 + 1] = gsrc * 1024 + 8 * pvec
                if j < c:
                    ohpad[:, (ui * 8 + j) * 16 + (b * 8 + j)] = 1.0
                else:
                    padflag[:, ui * 8 + j] = 1.0
            g0 = s - 1024
            rows = g0 + 16 * pvec
            bstart = b * 8192
            idxXA[:, ui * 4] = np.where(rows >= bstart, rows, ZR).astype(np.uint32)
            for r in range(2):
                rr = s - 1026 + r
                idxXB[2 * ui + r, 0] = rr if rr >= bstart else ZR
            def bxrow(bb, cc, within):
                rk = bb * 4 + cc // 2
                sl = cc % 2
                return rk * 10 + sl * 4 + within
            for r in range(2):
                idxH[ui * 4 + 0 + r, 0] = bxrow(b, c - 1, 2 + r) if c > 0 else i * 10 + 8 + r
                idxH[ui * 4 + 2 + r, 0] = bxrow(b, c + 1, 0 + r) if c < 7 else i * 10 + 8 + r
        mp.update({"bias2": bias2, "wmaskv": wm, "csqv": csq, "ssqv": ssq,
                   "cskv": csk, "sskv": ssk, "idxKV": idxKV, "idxXA": idxXA,
                   "idxXB": idxXB, "idxH": idxH, "ohpad": ohpad, "padflag": padflag})
        maps.append(mp)
    return maps


def kernel_fused(**inputs):
    inp = {k: np.asarray(v) for k, v in inputs.items()}
    nc = build_fused()
    maps = host_fused_maps(inp)
    res = _run(nc, maps)
    return _scatter_rows(res.results, "out")
